# version 01 (final)
# baseline (speedup 1.0000x reference)
import math
from collections import defaultdict
from contextlib import ExitStack

import numpy as np
import ml_dtypes

import concourse.bass as bass
import concourse.mybir as mybir
from concourse.bass_utils import run_bass_kernel_spmd

F32 = mybir.dt.float32
BF16 = mybir.dt.bfloat16
AF = mybir.ActivationFunctionType
ALU = mybir.AluOpType
AX = mybir.AxisListType

f32 = np.float32

D = 2048
L = 2048
NCORES = 8
NH = 16
DH = 128
DHR = 64
DC = 1024
INTER = 8192
TT = 512
NT = L // TT
NC = D // 128
NCC = DC // 128
NCI = INTER // 128
NQT = L // 128
EPS_LN = f32(1e-5)
EPS_RMS = f32(1e-6)
EXP_C = f32(-12.0)
MASKVAL = -1e30

SC_A = float((1.0 / 192.0) ** 0.25)
SC_B = float((1.0 / 128.0) ** 0.25)
LNC_2048 = float(np.log(1.0 / (64.0 * math.sqrt(2048.0))))
C_8192 = f32(1.0 / (64.0 * math.sqrt(8192.0)))

DEBUG_PHASE = None
DEV_CORES = None
A1_TTILES = L // TT
A1_PARTS = ("ln", "lam", "kr", "ckv")
SERIALIZE = False
SER_PAIRS = ()
HWDGE_W = False



def _hadamard(n):
    h = np.array([[1.0]], dtype=np.float64)
    while h.shape[0] < n:
        h = np.block([[h, h], [h, -h]])
    return h


def _host_consts():
    inv_freq = 1.0 / 10000.0 ** (np.arange(0, DHR, 2, dtype=np.float64) / DHR)
    freqs = np.arange(L, dtype=np.float64)[:, None] * inv_freq[None, :]
    emb = np.concatenate([freqs, freqs], axis=-1)
    cos = np.cos(emb).T
    sin = np.sin(emb).T
    sinp = sin.copy()
    sinp[0:32] *= -1.0
    cos2 = np.concatenate([cos, cos], axis=0) * SC_A
    sin2 = np.concatenate([sinp, sinp], axis=0) * SC_A
    masku = np.triu(np.full((128, 128), MASKVAL, dtype=np.float64), k=1)
    return dict(
        cos2=np.ascontiguousarray(cos2, dtype=np.float32),
        sin2=np.ascontiguousarray(sin2, dtype=np.float32),
        h128=_hadamard(128).astype(ml_dtypes.bfloat16),
        ident=np.eye(128, dtype=ml_dtypes.bfloat16),
        masku=masku.astype(ml_dtypes.bfloat16),
        ones_col_b=np.ones((128, 1), dtype=ml_dtypes.bfloat16),
        ones_row_f=np.ones((1, 128), dtype=np.float32),
    )


def _col_swap_rope(w):
    v = w.reshape(w.shape[0], w.shape[1] // 64, 2, 32)[:, :, ::-1, :]
    return np.ascontiguousarray(v.reshape(w.shape))


def _prep_inputs(inp):
    c = {}
    c["wdkv"] = np.ascontiguousarray(inp["L0_attn_a_W_DKV"])
    c["wdq"] = np.ascontiguousarray(inp["L0_attn_a_W_DQ"])
    c["kvnw"] = np.ascontiguousarray(inp["L0_attn_a_kv_norm_w"].reshape(1, DC))
    c["qnw"] = np.ascontiguousarray(inp["L0_attn_a_q_norm_w"].reshape(1, DC))
    c["wuk"] = np.ascontiguousarray(inp["L0_attn_a_W_UK"])
    c["wuv"] = np.ascontiguousarray(inp["L0_attn_a_W_UV"])
    c["wuq"] = np.ascontiguousarray(inp["L0_attn_a_W_UQ"])
    c["wqr"] = np.ascontiguousarray(inp["L0_attn_a_W_QR"])
    c["wqrs"] = _col_swap_rope(inp["L0_attn_a_W_QR"])
    c["wkr"] = np.ascontiguousarray(inp["L0_attn_a_W_KR"])
    c["wkrs"] = _col_swap_rope(inp["L0_attn_a_W_KR"])
    c["wlam"] = np.ascontiguousarray(inp["L0_attn_a_W_lambda_w"])
    c["wlamb"] = np.ascontiguousarray(inp["L0_attn_a_W_lambda_b"].reshape(1, NH))
    c["wout"] = np.ascontiguousarray(inp["L0_attn_a_W_out"])
    c["wqb"] = np.ascontiguousarray(inp["L0_attn_b_W_Q"].transpose(0, 2, 1))
    c["wkb"] = np.ascontiguousarray(inp["L0_attn_b_W_K"].transpose(0, 2, 1))
    c["wvb"] = np.ascontiguousarray(inp["L0_attn_b_W_V"].transpose(0, 2, 1))
    c["wob"] = np.ascontiguousarray(inp["L0_attn_b_W_O"].transpose(0, 2, 1))
    c["suug"] = np.ascontiguousarray(inp["L0_ffn_SU_ug"].reshape(1, D))
    c["sudp"] = np.ascontiguousarray(inp["L0_ffn_SU_dp"].reshape(NCI, 128).T)
    c["wsup"] = np.ascontiguousarray(
        inp["L0_ffn_Wscale_ug"][:INTER].reshape(NCI, 128).T)
    c["wsgate"] = np.ascontiguousarray(
        inp["L0_ffn_Wscale_ug"][INTER:].reshape(NCI, 128).T)
    c["wsdp"] = np.ascontiguousarray(inp["L0_ffn_Wscale_dp"].reshape(NC, 128).T)
    c["wupT"] = np.ascontiguousarray(inp["L0_ffn_W_up"].T)
    c["wgateT"] = np.ascontiguousarray(inp["L0_ffn_W_gate"].T)
    c["wdownT"] = np.ascontiguousarray(inp["L0_ffn_W_down"].T)
    return c



ENGINES = ("pe", "act", "dve", "pool", "sp")


class Op:
    __slots__ = ("eng", "fns", "waits", "inc_needed", "val", "n", "gid",
                 "lane")

    def __init__(self, eng, fns, n, gid, lane):
        self.eng = eng
        self.fns = fns
        self.waits = []
        self.inc_needed = False
        self.val = None
        self.n = n
        self.gid = gid
        self.lane = lane


class PB:

    def __init__(self, nc, es):
        self.nc = nc
        self.es = es
        self.ops = {e: [] for e in ENGINES}
        self.writer = {}
        self.readers = defaultdict(list)
        self.gid = 0
        self.rr = defaultdict(int)
        self.last_op = None
        self.serialize_local = False
        self.phase_bar = ()
        self.seen_res = set()

    NLANES = 4

    def emit(self, eng, fns, reads=(), writes=(), extra_waits=(), lane=None):
        if callable(fns):
            fns = [fns]
        self.gid += 1
        is_dma = eng in ("sp", "pool")
        if is_dma:
            if lane is None:
                lane = self.rr[eng] % self.NLANES
                self.rr[eng] += 1
            lane_name = f"{eng}{lane}"
        else:
            lane_name = eng
        op = Op(eng, fns, 16 if is_dma else 1, self.gid, lane_name)
        deps = []
        for r in reads:
            w = self.writer.get(r)
            if w is not None:
                deps.append(w)
            if isinstance(r, str) and r.startswith("ps"):
                for rd in self.readers.get(r, ()):
                    if rd.eng != eng:
                        deps.append(rd)
        for r in writes:
            w = self.writer.get(r)
            if w is not None:
                deps.append(w)
            deps.extend(self.readers.get(r, ()))
        for d in extra_waits:
            if d is not None:
                deps.append(d)
        if (SERIALIZE or self.serialize_local) and self.last_op is not None:
            deps.append(self.last_op)
        for pa, pb_ in SER_PAIRS:
            other = pb_ if eng == pa else (pa if eng == pb_ else None)
            if other is not None and self.ops[other]:
                deps.append(self.ops[other][-1])
        for r in writes:
            if r not in self.seen_res:
                self.seen_res.add(r)
                deps.extend(self.phase_bar)
        seen = set()
        for d in deps:
            if d is op or id(d) in seen:
                continue
            seen.add(id(d))
            d.inc_needed = True
            op.waits.append(d)
        for r in reads:
            self.readers[r].append(op)
        for r in writes:
            self.writer[r] = op
            self.readers[r] = []
        self.ops[eng].append(op)
        self.last_op = op
        return op

    def barrier_tokens(self):
        outs = []
        for e in ENGINES:
            if self.ops[e]:
                outs.append(self.ops[e][-1])
        return outs

    def flush(self):
        nc = self.nc
        lanes = set()
        for e in ENGINES:
            if e in ("sp", "pool"):
                prev_on_lane = {}
                for op in self.ops[e]:
                    op.inc_needed = True
                    p = prev_on_lane.get(op.lane)
                    if p is not None and p not in op.waits:
                        op.waits.append(p)
                    prev_on_lane[op.lane] = op
            cum = defaultdict(int)
            for op in self.ops[e]:
                if op.inc_needed:
                    cum[op.lane] += op.n
                    op.val = cum[op.lane]
                    lanes.add(op.lane)
        sems = {ln: self.es.enter_context(nc.semaphore(f"sem_{ln}"))
                for ln in lanes}

        all_ops = sorted((op for e in ENGINES for op in self.ops[e]),
                        key=lambda o: o.gid)
        DMA_ENGS = ("sp", "pool")
        issue_imp = {}
        comp_imp = {}
        eng_prev = {}
        for op in all_ops:
            prev = eng_prev.get(op.eng)
            issue = dict(issue_imp[prev]) if prev is not None else {}
            need = {}
            for d in op.waits:
                assert d.val is not None, "dep without inc"
                if d.val > need.get(d.lane, (0, None))[0]:
                    need[d.lane] = (d.val, d)
            kept = {}
            for lane_, (v, d) in sorted(need.items()):
                ci = comp_imp[d]
                if issue.get(lane_, 0) < v:
                    kept[lane_] = v
                for l2, v2 in ci.items():
                    if issue.get(l2, 0) < v2:
                        issue[l2] = v2
                if issue.get(lane_, 0) < v:
                    issue[lane_] = v
            op.waits = kept
            issue_imp[op] = issue
            comp = dict(issue)
            if op.inc_needed:
                if comp.get(op.lane, 0) < op.val:
                    comp[op.lane] = op.val
            if prev is not None and op.eng not in DMA_ENGS:
                for l2, v2 in comp_imp[prev].items():
                    if comp.get(l2, 0) < v2:
                        comp[l2] = v2
            comp_imp[op] = comp
            eng_prev[op.eng] = op

        def runner(eng_name):
            eng_ops = self.ops[eng_name]

            def _run(eng):
                water = defaultdict(int)
                for op in eng_ops:
                    for s, v in op.waits.items():
                        if v > water[s]:
                            eng.wait_ge(sems[s], v)
                            water[s] = v
                    inst = None
                    for fn in op.fns:
                        inst = fn(eng)
                    if op.inc_needed:
                        inst.then_inc(sems[op.lane], op.n)
            return _run

        with nc.Block() as block:
            block.tensor(runner("pe"))
            block.scalar(runner("act"))
            block.vector(runner("dve"))
            block.gpsimd(runner("pool"))
            block.sync(runner("sp"))


class Arena:
    def __init__(self, total):
        self.free = [(0, total)]
        self.named = {}
        self.on_alloc = None

    def alloc(self, name, cols):
        for i, (off, sz) in enumerate(self.free):
            if sz >= cols:
                self.named[name] = (off, cols)
                if sz == cols:
                    self.free.pop(i)
                else:
                    self.free[i] = (off + cols, sz - cols)
                if self.on_alloc is not None:
                    self.on_alloc()
                return off
        raise MemoryError(f"arena OOM: {name} needs {cols} cols; free={self.free}")

    def release(self, name):
        off, cols = self.named.pop(name)
        self.free.append((off, cols))
        self.free.sort()
        merged = []
        for o, s in self.free:
            if merged and merged[-1][0] + merged[-1][1] == o:
                merged[-1] = (merged[-1][0], merged[-1][1] + s)
            else:
                merged.append((o, s))
        self.free = merged



def build_program(debug_phase=None):
    es = ExitStack()
    nc = bass.Bass("TRN2", target_bir_lowering=False)
    pb = PB(nc, es)

    dx = nc.dram_tensor("x_fm", [D, L], F32, kind="ExternalInput").ap()
    dm = {}
    for name, shape in [
        ("wdkv", (D, DC)), ("wdq", (D, DC)), ("kvnw", (1, DC)), ("qnw", (1, DC)),
        ("wuk", (DC, D)), ("wuv", (DC, D)), ("wuq", (DC, 2 * D)),
        ("wqr", (DC, D)), ("wqrs", (DC, D)), ("wkr", (D, DHR)), ("wkrs", (D, DHR)),
        ("wlam", (D, NH)), ("wlamb", (1, NH)), ("wout", (D, D)),
        ("wqb", (NH, D, DH)), ("wkb", (NH, D, DH)), ("wvb", (NH, D, DH)),
        ("wob", (NH, DH, D)),
        ("suug", (1, D)), ("sudp", (128, NCI)), ("wsup", (128, NCI)),
        ("wsgate", (128, NCI)), ("wsdp", (128, NC)),
        ("wupT", (D, INTER)), ("wgateT", (D, INTER)), ("wdownT", (INTER, D)),
        ("cos2", (128, L)), ("sin2", (128, L)), ("ones_row_f", (1, 128)),
    ]:
        dm[name] = nc.dram_tensor(name, list(shape), F32,
                                  kind="ExternalInput").ap()
    for name in ("h128", "ident", "masku", "ones_col_b"):
        shp = [128, 1] if name == "ones_col_b" else [128, 128]
        dm[name] = nc.dram_tensor(name, shp, BF16, kind="ExternalInput").ap()

    d_stream = nc.dram_tensor("scr_stream", [NC, 128, L], F32,
                              kind="Internal").ap()
    d_spill = nc.dram_tensor("scr_spill", [NH, 128, L], BF16,
                             kind="Internal").ap()
    d_wdp = nc.dram_tensor("scr_wdp", [NCI, 128, D], BF16, kind="Internal").ap()
    dy = nc.dram_tensor("y_fm", [D, L], F32, kind="ExternalOutput").ap()

    dbg_handles = {}

    def dbg(name, shape, dtype=F32):
        ap = nc.dram_tensor(name, list(shape), dtype, kind="ExternalOutput").ap()
        dbg_handles[name] = ap
        return ap

    sb = {}
    for name, shape, dt in [
        ("c_h128", [128, 128], BF16), ("c_ident", [128, 128], BF16),
        ("c_masku", [128, 128], BF16), ("c_onesb", [128, 1], BF16),
        ("c_onesf", [1, 128], F32), ("c_suug", [1, D], F32),
        ("c_kvnw", [1, DC], F32), ("c_qnw", [1, DC], F32),
        ("c_wlamb", [1, NH], F32),
        ("c_sudp", [128, NCI], F32), ("c_wsup", [128, NCI], F32),
        ("c_wsgate", [128, NCI], F32), ("c_wsdp", [128, NC], F32),
        ("c_wsg64", [128, NCI], F32), ("c_wsups", [128, NCI], F32),
        ("c_wsdp64", [128, NC], F32),
        ("r_mean", [1, TT], F32), ("r_msq", [1, TT], F32),
        ("r_var", [1, TT], F32), ("r_lnv", [1, TT], F32),
        ("r_rstd", [1, TT], F32), ("r_tmp", [1, TT], F32),
        ("lam_sb", [128, NQT * NH], F32),
        ("scol", [128, 64], F32), ("rs1", [128, 16], F32),
        ("rs2", [128, 16], F32), ("exp_t", [128, NH], F32),
        ("exp_t2", [128, NH], F32), ("lamb_b", [128, NH], F32),
        ("krT", [64, L], BF16),
        ("qrT0", [64, L], BF16), ("qrT1", [64, L], BF16),
    ]:
        sb[name] = nc.alloc_sbuf_tensor(name, shape, dt)

    psf = [nc.alloc_psum_tensor(f"ps{i}", [128, 512], F32) for i in range(8)]
    psb = [p.bitcast(BF16) for p in psf]
    PS = [f"ps{i}" for i in range(8)]

    rem = nc.sbuf_bytes_remaining
    total_cols = rem // 4 - 16
    arena = nc.alloc_sbuf_tensor("arena", [128, total_cols], F32)
    arena_b = arena.bitcast(BF16)
    AR = Arena(total_cols)

    def _on_alloc():
        pb.phase_bar = pb.barrier_tokens()
        pb.seen_res = set()
    AR.on_alloc = _on_alloc

    def fv(off, cols, p0=0, p1=128):
        return arena.ap()[p0:p1, off:off + cols]

    def bv(off, bcols, p0=0, p1=128):
        return arena_b.ap()[p0:p1, 2 * off:2 * off + bcols]

    def mm(out, lhsT, rhs, start, stop, reads=(), writes=(), ew=()):
        return pb.emit("pe", lambda e, o=out, l=lhsT, r=rhs, a=start, b=stop:
                       e.matmul(o, l, r, start=a, stop=b),
                       reads=reads, writes=writes, extra_waits=ew)

    def mmgroup(terms, reads=(), writes=(), ew=()):
        fns = [
            (lambda e, o=o, l=l, r=r, a=a, b=b: e.matmul(o, l, r, start=a,
                                                         stop=b))
            for (o, l, r, a, b) in terms]
        return pb.emit("pe", fns, reads=reads, writes=writes, extra_waits=ew)

    def trp(out, in_, reads=(), writes=(), ew=()):
        return pb.emit("pe", lambda e, o=out, i=in_, d=sb["c_ident"].ap():
                       e.transpose(o, i, d), reads=tuple(reads) + ("c_ident",),
                       writes=writes, extra_waits=ew)

    def av(out, in_, func, bias=f32(0), scale=f32(1), accum=None,
           reads=(), writes=(), ew=()):
        return pb.emit("act", lambda e, o=out, i=in_, f=func, b=bias, s=scale,
                       a=accum: e.activation(o, i, f, bias=b, scale=s,
                                             accum_out=a),
                       reads=reads, writes=writes, extra_waits=ew)

    def dcopy(out, in_, reads=(), writes=(), ew=()):
        return pb.emit("dve", lambda e, o=out, i=in_: e.tensor_copy(o, i),
                       reads=reads, writes=writes, extra_waits=ew)

    def dtt(out, a, b, op, reads=(), writes=(), ew=()):
        return pb.emit("dve", lambda e, o=out, x=a, y=b, p=op:
                       e.tensor_tensor(o, x, y, p),
                       reads=reads, writes=writes, extra_waits=ew)

    def dts(out, a, s1, s2, op0, op1=ALU.bypass, accum=None,
            reads=(), writes=(), ew=()):
        if isinstance(s1, (int, float, np.floating)):
            s1 = float(s1)
        if isinstance(s2, (int, float, np.floating)):
            s2 = float(s2)
        return pb.emit("dve", lambda e, o=out, x=a, u=s1, v=s2, p0=op0, p1=op1,
                       ac=accum: e.tensor_scalar(o, x, u, v, p0, p1,
                                                 accum_out=ac),
                       reads=reads, writes=writes, extra_waits=ew)

    def dstt(out, in0, scal, in1, op0, op1, reads=(), writes=(), ew=()):
        return pb.emit("dve", lambda e, o=out, a=in0, s=scal, b=in1, p0=op0,
                       p1=op1: e.scalar_tensor_tensor(o, a, s, b, p0, p1),
                       reads=reads, writes=writes, extra_waits=ew)

    def dred(out, in_, op, reads=(), writes=(), ew=()):
        return pb.emit("dve", lambda e, o=out, i=in_, p=op:
                       e.tensor_reduce(o, i, AX.X, p),
                       reads=reads, writes=writes, extra_waits=ew)

    def drecip(out, in_, reads=(), writes=(), ew=()):
        return pb.emit("dve", lambda e, o=out, i=in_: e.reciprocal(o, i),
                       reads=reads, writes=writes, extra_waits=ew)

    def ld(dst, src, reads=(), writes=(), eng="sp", ew=()):
        return pb.emit(eng, lambda e, d=dst, s=src: e.dma_start(d, s),
                       reads=reads, writes=writes, extra_waits=ew)

    for nm, src in [("c_h128", "h128"),
                    ("c_ident", "ident"), ("c_masku", "masku"),
                    ("c_onesb", "ones_col_b"), ("c_onesf", "ones_row_f"),
                    ("c_suug", "suug"), ("c_kvnw", "kvnw"), ("c_qnw", "qnw"),
                    ("c_wlamb", "wlamb"), ("c_sudp", "sudp"),
                    ("c_wsup", "wsup"), ("c_wsgate", "wsgate"),
                    ("c_wsdp", "wsdp")]:
        ld(sb[nm].ap(), dm[src], writes=(nm,))
    multi_o = AR.alloc("multi", 17 * 256)
    cos2_ap = fv(multi_o, L)
    sin2_ap = fv(multi_o + L, L)
    ld(cos2_ap, dm["cos2"], writes=("multi",))
    ld(sin2_ap, dm["sin2"], writes=("multi",))
    dts(sb["c_wsg64"].ap(), sb["c_wsgate"].ap(), f32(64.0), None, ALU.mult,
        reads=("c_wsgate",), writes=("c_wsg64",))
    dtt(sb["c_wsups"].ap(), sb["c_wsup"].ap(), sb["c_sudp"].ap(), ALU.mult,
        reads=("c_wsup", "c_sudp"), writes=("c_wsups",))
    dts(sb["c_wsups"].ap(), sb["c_wsups"].ap(), f32(64.0), None, ALU.mult,
        reads=("c_wsups",), writes=("c_wsups",))
    dts(sb["c_wsdp64"].ap(), sb["c_wsdp"].ap(), f32(64.0), None, ALU.mult,
        reads=("c_wsdp",), writes=("c_wsdp64",))

    onesb = sb["c_onesb"].ap()
    onesf = sb["c_onesf"].ap()

    mm(psf[5].ap()[:, 0:NH], onesf, sb["c_wlamb"].ap(), True, True,
       reads=("c_onesf", "c_wlamb"), writes=("ps5",))
    dcopy(sb["lamb_b"].ap(), psf[5].ap()[:, 0:NH], reads=("ps5",),
          writes=("lamb_b",))

    def ln_stats(src_o, tn, nch, inv_n, eps, extra_lnbias, tag, xb_at=None,
                 sq_at=None, xtag=None, stag=None):
        xo = xbln_o if xb_at is None else xb_at
        so = sq_o if sq_at is None else sq_at
        xtag = xtag or "xbln"
        stag = stag or "sq"
        xb_w = bv(xo, nch * tn)
        dcopy(xb_w, fv(src_o, nch * tn), reads=(tag,), writes=(xtag,))
        sq_w = bv(so, nch * tn)
        av(sq_w, xb_w, AF.Square, reads=(xtag,), writes=(stag,))
        stp1 = psf[6].ap()[0:1, 0:tn]
        stp2 = psf[7].ap()[0:1, 0:tn]
        mmgroup([(stp1, onesb, bv(xo + c * tn // 2, tn), c == 0,
                  c == nch - 1) for c in range(nch)],
                reads=(xtag, "c_onesb"), writes=("ps6",))
        mmgroup([(stp2, onesb, bv(so + c * tn // 2, tn), c == 0,
                  c == nch - 1) for c in range(nch)],
                reads=(stag, "c_onesb"), writes=("ps7",))
        r_mean = sb["r_mean"].ap()[:, 0:tn]
        r_msq = sb["r_msq"].ap()[:, 0:tn]
        r_var = sb["r_var"].ap()[:, 0:tn]
        r_lnv = sb["r_lnv"].ap()[:, 0:tn]
        r_rstd = sb["r_rstd"].ap()[:, 0:tn]
        r_tmp = sb["r_tmp"].ap()[:, 0:tn]
        dts(r_mean, stp1, f32(inv_n), None, ALU.mult, reads=("ps6",),
            writes=("r_mean",))
        dts(r_tmp, stp2, f32(inv_n), None, ALU.mult, reads=("ps7",),
            writes=("r_tmp",))
        av(r_msq, r_mean, AF.Square, reads=("r_mean",), writes=("r_msq",))
        dtt(r_var, r_tmp, r_msq, ALU.subtract, reads=("r_tmp", "r_msq"),
            writes=("r_var",))
        av(r_lnv, r_var, AF.Ln, bias=eps, reads=("r_var",), writes=("r_lnv",))
        av(r_rstd, r_lnv, AF.Exp, bias=f32(extra_lnbias), scale=f32(-0.5),
           reads=("r_lnv",), writes=("r_rstd",))

    def bcast_row(row_res, row_ap, tn, dst_off, bank, tag_dst, lhs_ap=None,
                  lhs_res=()):
        lhs = lhs_ap if lhs_ap is not None else onesf
        mm(psf[bank].ap()[:, 0:tn], lhs, row_ap, True, True,
           reads=tuple(lhs_res) + ("c_onesf", row_res), writes=(PS[bank],))
        dcopy(fv(dst_off, tn), psf[bank].ap()[:, 0:tn],
              reads=(PS[bank],), writes=(tag_dst,))

    ckvn_o = AR.alloc("ckvn", NCC * L // 2)
    cqn_o = AR.alloc("cqn", NCC * L // 2)

    def ckvn_ap(c, n0, n1):
        return arena_b.ap()[:, 2 * ckvn_o + c * L + n0: 2 * ckvn_o + c * L + n1]

    def cqn_ap(c, n0, n1):
        return arena_b.ap()[:, 2 * cqn_o + c * L + n0: 2 * cqn_o + c * L + n1]

    def krT_ap(n0, n1):
        return sb["krT"].ap()[:, n0:n1]

    xt_o = AR.alloc("xt", NC * TT)
    xbln_o = AR.alloc("xbln", NC * TT // 2)
    sq_o = AR.alloc("sq", NC * TT // 2)
    ckvf_o = xt_o
    wstr_o = AR.alloc("wstr", 2 * DC // 2)
    wstrf_o = AR.alloc("wstrf", DC) if HWDGE_W else None
    wkr_o = AR.alloc("wkr", NC * DHR)
    wlamA_o = AR.alloc("wlamA", NC * NH // 2)
    mb_o = AR.alloc("meanb", TT)
    rb_o = AR.alloc("rstdb", TT)
    tmpA_o = AR.alloc("tmpA", 2 * TT)

    def ln_bf(c, n0=0, n1=TT):
        return arena_b.ap()[:, 2 * xbln_o + c * TT + n0:
                            2 * xbln_o + c * TT + n1]

    ld(bv(wkr_o, NC * DHR).rearrange("p (c r) -> p c r", c=NC),
       dm["wkr"].rearrange("(c p) r -> p c r", p=128),
       eng="pool", writes=("wkr",))
    ld(arena_b.ap()[:, 2 * wkr_o + NC * DHR: 2 * wkr_o + 2 * NC * DHR]
       .rearrange("p (c r) -> p c r", c=NC),
       dm["wkrs"].rearrange("(c p) r -> p c r", p=128), eng="pool",
       writes=("wkrs",))
    ld(bv(wlamA_o, NC * NH).rearrange("p (c h) -> p c h", c=NC),
       dm["wlam"].rearrange("(c p) h -> p c h", p=128),
       eng="pool", writes=("wlamA",))

    def emit_a1():
        for t in range(A1_TTILES):
            t0, t1 = t * TT, (t + 1) * TT
            ld(fv(xt_o, NC * TT).rearrange("p (c n) -> p c n", c=NC),
               dx.rearrange("(c p) l -> p c l", p=128)[:, :, t0:t1],
               writes=("xt",))
            ln_stats(xt_o, TT, NC, 1.0 / D, EPS_LN, 0.0, "xt")
            bcast_row("r_mean", sb["r_mean"].ap()[:, 0:TT], TT, mb_o, 6,
                      "meanb")
            bcast_row("r_rstd", sb["r_rstd"].ap()[:, 0:TT], TT, rb_o, 7,
                      "rstdb")
            xt3 = fv(xt_o, NC * TT).rearrange("p (c n) -> p c n", c=NC)
            xb3 = bv(xbln_o, NC * TT).rearrange("p (c n) -> p c n", c=NC)
            mrep = bass.broadcast_tensor_aps(
                fv(mb_o, TT).rearrange("p (o n) -> p o n", o=1), xt3)[0]
            rrep = bass.broadcast_tensor_aps(
                fv(rb_o, TT).rearrange("p (o n) -> p o n", o=1), xt3)[0]
            dtt(xt3, xt3, mrep, ALU.subtract, reads=("xt", "meanb"),
                writes=("xt",))
            dtt(xb3, xt3, rrep, ALU.mult, reads=("xt", "rstdb"),
                writes=("xbln",))
            lam_ps = psf[5].ap()[:, 0:NH]
            for qt in range(4 if "lam" in A1_PARTS else 0):
                q0 = qt * 128
                terms = [(lam_ps, ln_bf(c, q0, q0 + 128),
                          bv(wlamA_o + c * NH // 2, NH), c == 0, c == NC - 1)
                         for c in range(NC)]
                mmgroup(terms, reads=("xbln", "wlamA"), writes=("ps5",))
                dtt(sb["exp_t"].ap(), lam_ps, sb["lamb_b"].ap(), ALU.add,
                    reads=("ps5", "lamb_b"), writes=("exp_t",))
                av(sb["exp_t2"].ap(), sb["exp_t"].ap(), AF.Exp, scale=f32(-1),
                   reads=("exp_t",), writes=("exp_t2",))
                qi = t * 4 + qt
                lam_cols = sb["lam_sb"].ap()[:, qi * NH:(qi + 1) * NH]
                dts(lam_cols, sb["exp_t2"].ap(), f32(1.0), None, ALU.add,
                    reads=("exp_t2",), writes=(("lam", qi),))
                drecip(lam_cols, lam_cols, reads=(("lam", qi),),
                       writes=(("lam", qi),))
            if "kr" not in A1_PARTS:
                continue
            krp = psf[4].ap()[0:64, 0:TT]
            krps = psf[5].ap()[0:64, 0:TT]
            mmgroup([(krp,
                      arena_b.ap()[:, 2 * wkr_o + c * DHR:
                                   2 * wkr_o + (c + 1) * DHR],
                      ln_bf(c), c == 0, c == NC - 1) for c in range(NC)],
                    reads=("xbln", "wkr"), writes=("ps4",))
            mmgroup([(krps,
                      arena_b.ap()[:, 2 * wkr_o + NC * DHR + c * DHR:
                                   2 * wkr_o + NC * DHR + (c + 1) * DHR],
                      ln_bf(c), c == 0, c == NC - 1) for c in range(NC)],
                    reads=("xbln", "wkrs"), writes=("ps5",))
            if "kr_nodve" in A1_PARTS:
                continue
            ta = fv(tmpA_o, TT, 0, 64)
            tb = fv(tmpA_o + TT, TT, 0, 64)
            if "kr_nomulti" in A1_PARTS:
                cosk = fv(mb_o, TT, 0, 64)
                sink = fv(mb_o, TT, 0, 64)
            else:
                cosk = fv(multi_o + t0, TT, 0, 64)
                sink = fv(multi_o + L + t0, TT, 0, 64)
            pb.serialize_local = "kr_ser" in A1_PARTS
            if "kr_nopshift" in A1_PARTS:
                t1_ = pb.emit("dve", lambda e: e.engine_nop())
                t1_.inc_needed = True
                t2_ = pb.emit("dve", lambda e: e.engine_nop())
                t2_.inc_needed = True
                t3_ = pb.emit("act", lambda e: e.nop())
                t3_.inc_needed = True
                pb.serialize_local = False
                continue
            if "kr_act" in A1_PARTS:
                ea = fv(tmpA_o, TT, 0, 64)
                eb = fv(tmpA_o + TT, TT, 0, 64)
                av(ea, krp, AF.Identity, reads=("ps4",), writes=("tmpA_lo",))
                av(eb, krps, AF.Identity, reads=("ps5",), writes=("tmpA_hi",))
                if "kr_actonly" in A1_PARTS:
                    pb.serialize_local = False
                    continue
                dtt(ea, ea, cosk, ALU.mult, reads=("tmpA_lo", "multi"),
                    writes=("tmpA_lo",))
                dtt(eb, eb, sink, ALU.mult, reads=("tmpA_hi", "multi"),
                    writes=("tmpA_hi",))
                if "kr_no3" in A1_PARTS:
                    pb.serialize_local = False
                    continue
                dtt(ea, ea, eb, ALU.add, reads=("tmpA_lo", "tmpA_hi"),
                    writes=("tmpA_lo",))
                if "kr_addonly" in A1_PARTS:
                    pb.serialize_local = False
                    continue
                av(krT_ap(t0, t1), ea, AF.Identity, reads=("tmpA_lo",),
                   writes=("krT",))
            else:
                dtt(ta, krp, cosk, ALU.mult, reads=("ps4", "multi"),
                    writes=("tmpA_lo",))
                dtt(tb, krps, sink, ALU.mult, reads=("ps5", "multi"),
                    writes=("tmpA_hi",))
                dtt(krT_ap(t0, t1), ta, tb, ALU.add,
                    reads=("tmpA_lo", "tmpA_hi"), writes=("krT",))
            if "kr_pad" in A1_PARTS:
                pb.emit("dve", lambda e: e.engine_nop(),
                        reads=(), writes=("ps4", "ps5"))
            pb.serialize_local = False
            if "ckv" not in A1_PARTS:
                continue
            for wname, nrow, dstfn, dsttag in (
                ("wdkv", "c_kvnw", ckvn_ap, "ckvn"),
                ("wdq", "c_qnw", cqn_ap, "cqn"),
            ):
                for k in range(NC):
                    slot = k % 2
                    woff = wstr_o + slot * DC // 2
                    wres = ("wstr", slot)
                    if HWDGE_W:
                        fres = ("wstrf", 0)
                        foff = wstrf_o
                        ld(fv(foff, DC), dm[wname][k * 128:(k + 1) * 128, :],
                           eng="sp", writes=(fres,))
                        av(bv(woff, DC), fv(foff, DC), AF.Identity,
                           reads=(fres,), writes=(wres,))
                    else:
                        ld(bv(woff, DC), dm[wname][k * 128:(k + 1) * 128, :],
                           eng="pool", writes=(wres,))
                    if "ckv_dma_only" in A1_PARTS:
                        continue
                    mmgroup([(psf[oc].ap(), bv(woff + oc * 64, 128), ln_bf(k),
                              k == 0, k == NC - 1) for oc in range(NCC)],
                            reads=("xbln", wres),
                            writes=tuple(PS[i] for i in range(NCC)))
                if "ckv_mm_only" in A1_PARTS:
                    continue
                for oc in range(NCC):
                    dcopy(fv(ckvf_o + oc * TT, TT), psf[oc].ap(),
                          reads=(PS[oc],), writes=("xt",))
                    av(bv(sq_o + oc * TT // 2, TT), fv(ckvf_o + oc * TT, TT),
                       AF.Square, reads=("xt",), writes=(("sqo", oc),))
                if "ckv_evac_only" in A1_PARTS:
                    continue
                mmgroup([(psf[6].ap()[0:1, 0:TT], onesb,
                          bv(sq_o + oc * TT // 2, TT), oc == 0, oc == NCC - 1)
                         for oc in range(NCC)],
                        reads=tuple(("sqo", oc) for oc in range(NCC))
                        + ("c_onesb",), writes=("ps6",))
                r_var = sb["r_var"].ap()[:, 0:TT]
                r_lnv = sb["r_lnv"].ap()[:, 0:TT]
                r_rstd = sb["r_rstd"].ap()[:, 0:TT]
                dts(r_var, psf[6].ap()[0:1, 0:TT], f32(1.0 / DC), None,
                    ALU.mult, reads=("ps6",), writes=("r_var",))
                av(r_lnv, r_var, AF.Ln, bias=EPS_RMS, reads=("r_var",),
                   writes=("r_lnv",))
                av(r_rstd, r_lnv, AF.Exp, scale=f32(-0.5), reads=("r_lnv",),
                   writes=("r_rstd",))
                for oc in range(NCC):
                    mm(psf[oc].ap(), sb[nrow].ap()[:, oc * 128:(oc + 1) * 128],
                       sb["r_rstd"].ap()[:, 0:TT], True, True,
                       reads=("r_rstd", nrow), writes=(PS[oc],))
                for oc in range(NCC):
                    dtt(dstfn(oc, t0, t1), fv(ckvf_o + oc * TT, TT),
                        psf[oc].ap(), ALU.mult,
                        reads=("xt", PS[oc]), writes=(dsttag,))
        for nm in ("wstr", "wlamA", "wkr"):
            AR.release(nm)

    emit_a1()

    def emit_a2_real():
        for nm in (("xt", "xbln", "sq", "wstrf", "meanb", "rstdb", "tmpA")
                   if HWDGE_W else
                   ("xt", "xbln", "sq", "meanb", "rstdb", "tmpA")):
            AR.release(nm)
        pb.phase_bar = pb.barrier_tokens()

        kcT_o = AR.alloc("kcT", L // 2)
        vfm_o = AR.alloc("vfm", L // 2)
        vtok_o = AR.alloc("vtok", L // 2)
        qcT_o = AR.alloc("qcT", L)
        p1_o = AR.alloc("p1b", L)
        p2_o = AR.alloc("p2b", L)
        pc_o = AR.alloc("pcb", L)
        t2_o = AR.alloc("t2b", L // 2)
        pT_o = AR.alloc("pTb", L)
        aT_o = AR.alloc("aTh", L // 2)
        rt_o = AR.alloc("ropet", 2 * TT)
        wA_o = AR.alloc("wA", 3072)

        def ba(off_f32, bcol0, bn, p0=0, p1=128):
            c = 2 * off_f32 + bcol0
            return arena_b.ap()[p0:p1, c:c + bn]

        for h in range(NH):
            slot = 0
            so = wA_o
            wuk_s, wuv_s, wuq_s, wqr_s, wqrs_s = (so, so + 512, so + 1024,
                                                  so + 2048, so + 2560)
            res_w = ("wA", slot)
            for nm, off, cols in (("wuk", wuk_s, 128), ("wuv", wuv_s, 128),
                                  ("wuq", wuq_s, 256), ("wqr", wqr_s, 128),
                                  ("wqrs", wqrs_s, 128)):
                src = dm[nm].rearrange("(k p) n -> p k n", p=128)
                c0 = h * cols
                ld(ba(off, 0, NCC * cols).rearrange("p (k n) -> p k n", k=NCC),
                   src[:, :, c0:c0 + cols], eng="pool", writes=(res_w,))
            for t in range(NT):
                t0, t1 = t * TT, (t + 1) * TT
                mmgroup([(psf[4].ap(), ba(wuk_s, k * 128, 128),
                          ckvn_ap(k, t0, t1), k == 0, k == NCC - 1)
                         for k in range(NCC)],
                        reads=("ckvn", res_w), writes=("ps4",))
                av(ba(kcT_o, t0, TT), psf[4].ap(), AF.Identity,
                   scale=f32(SC_A), reads=("ps4",), writes=("kcT",))
                mmgroup([(psf[5].ap(), ba(wuv_s, k * 128, 128),
                          ckvn_ap(k, t0, t1), k == 0, k == NCC - 1)
                         for k in range(NCC)],
                        reads=("ckvn", res_w), writes=("ps5",))
                av(ba(vfm_o, t0, TT), psf[5].ap(), AF.Identity,
                   reads=("ps5",), writes=("vfm",))
                for p_ in range(2):
                    mmgroup([(psf[6 + p_].ap(), ba(wuq_s, k * 256 + p_ * 128, 128),
                              cqn_ap(k, t0, t1), k == 0, k == NCC - 1)
                             for k in range(NCC)],
                            reads=("cqn", res_w), writes=(PS[6 + p_],))
                    av(ba(qcT_o, p_ * L + t0, TT), psf[6 + p_].ap(),
                       AF.Identity, scale=f32(SC_A),
                       reads=(PS[6 + p_],), writes=("qcT",))
                for p_ in range(2):
                    qps = psf[4].ap()[0:64, 0:TT]
                    qpss = psf[5].ap()[0:64, 0:TT]
                    mmgroup([(qps, ba(wqr_s, k * 128 + p_ * 64, 64),
                              cqn_ap(k, t0, t1), k == 0, k == NCC - 1)
                             for k in range(NCC)],
                            reads=("cqn", res_w), writes=("ps4",))
                    mmgroup([(qpss, ba(wqrs_s, k * 128 + p_ * 64, 64),
                              cqn_ap(k, t0, t1), k == 0, k == NCC - 1)
                             for k in range(NCC)],
                            reads=("cqn", res_w), writes=("ps5",))
                    ra = fv(rt_o, TT, 0, 64)
                    rb2 = fv(rt_o + TT, TT, 0, 64)
                    cosq = fv(multi_o + t0, TT, 0, 64)
                    sinq = fv(multi_o + L + t0, TT, 0, 64)
                    dtt(ra, qps, cosq, ALU.mult, reads=("ps4", "multi"),
                        writes=("ropea",))
                    dtt(rb2, qpss, sinq, ALU.mult, reads=("ps5", "multi"),
                        writes=("ropeb",))
                    qrt = (sb["qrT0"] if p_ == 0 else sb["qrT1"]).ap()
                    dtt(qrt[:, t0:t1], ra, rb2, ALU.add,
                        reads=("ropea", "ropeb"), writes=(("qrT", p_),))
            for mt8 in range(2):
                for mi in range(8):
                    mt = mt8 * 8 + mi
                    trp(psb[2].ap()[:, mi * 128: mi * 128 + 128],
                        ba(vfm_o, mt * 128, 128),
                        reads=("vfm",), writes=("ps2",))
                dcopy(ba(vtok_o, mt8 * 1024, 1024), psb[2].ap(),
                      reads=("ps2",), writes=("vtok",))
            for g in range(NQT // 2):
                qts = (2 * g, 2 * g + 1)
                for p_ in range(2):
                    pbuf_o = p1_o if p_ == 0 else p2_o
                    pres = ("p1" if p_ == 0 else "p2")
                    qrt = (sb["qrT0"] if p_ == 0 else sb["qrT1"]).ap()
                    for qi, qt in enumerate(qts):
                        m_end = (qt + 1) * 128
                        nmb = (m_end + 511) // 512
                        for mb in range(nmb):
                            n = min(512, m_end - mb * 512)
                            bank = mb % 2
                            terms = [
                                (psf[bank].ap()[:, 0:n],
                                 ba(qcT_o, p_ * L + qt * 128, 128),
                                 ba(kcT_o, mb * 512, n), True, False),
                                (psf[bank].ap()[:, 0:n],
                                 qrt[:, qt * 128:(qt + 1) * 128],
                                 sb["krT"].ap()[:, mb * 512: mb * 512 + n],
                                 False, False),
                            ]
                            dg0 = qt * 128 - mb * 512
                            if 0 <= dg0 < n:
                                terms.append(
                                    (psf[bank].ap()[:, dg0:dg0 + 128],
                                     sb["c_ident"].ap(), sb["c_masku"].ap(),
                                     False, True))
                            else:
                                o_, l_, r_, s_, _ = terms[-1]
                                terms[-1] = (o_, l_, r_, s_, True)
                            mmgroup(terms,
                                    reads=("qcT", ("qrT", p_), "kcT", "krT",
                                           "c_ident", "c_masku"),
                                    writes=(PS[bank],))
                            sc_col = sb["scol"].ap()[:, p_ * 8 + qi * 4 + mb:
                                                     p_ * 8 + qi * 4 + mb + 1]
                            av(ba(pbuf_o, qi * L + mb * 512, n),
                               psf[bank].ap()[:, 0:n], AF.Exp, bias=EXP_C,
                               accum=sc_col, reads=(PS[bank],),
                               writes=(pres, ("scol", p_, qi, mb)))
                for p_ in range(2):
                    rsb = (sb["rs1"] if p_ == 0 else sb["rs2"]).ap()
                    for qi, qt in enumerate(qts):
                        nmb = ((qt + 1) * 128 + 511) // 512
                        sc0 = p_ * 8 + qi * 4
                        dred(rsb[:, qi:qi + 1],
                             sb["scol"].ap()[:, sc0:sc0 + nmb], ALU.add,
                             reads=tuple(("scol", p_, qi, m)
                                         for m in range(nmb)),
                             writes=(("rs", p_, qi),))
                        drecip(rsb[:, qi:qi + 1], rsb[:, qi:qi + 1],
                               reads=(("rs", p_, qi),),
                               writes=(("rs", p_, qi),))
                for qi, qt in enumerate(qts):
                    m_end = (qt + 1) * 128
                    lam_col = sb["lam_sb"].ap()[:, qt * NH + h: qt * NH + h + 1]
                    dts(ba(t2_o, 0, m_end), ba(p2_o, qi * L, m_end),
                        lam_col, sb["rs2"].ap()[:, qi:qi + 1], ALU.mult,
                        ALU.mult,
                        reads=("p2", ("rs", 1, qi), ("lam", qt)),
                        writes=("t2b",))
                    dstt(ba(pc_o, qi * L, m_end), ba(p1_o, qi * L, m_end),
                         sb["rs1"].ap()[:, qi:qi + 1], ba(t2_o, 0, m_end),
                         ALU.mult, ALU.subtract,
                         reads=("p1", ("rs", 0, qi), "t2b"),
                         writes=(("pc", qi),))
                hi_mt = 2 * g + 1
                for mt4 in range((hi_mt + 4) // 4):
                    lo = mt4 * 4
                    hi = min(lo + 4, hi_mt + 1)
                    for mt in range(lo, hi):
                        for qi, qt in enumerate(qts):
                            src_qi = qi if mt <= qt else 1
                            trp(psb[2].ap()[:, (mt - lo) * 256 + qi * 128:
                                            (mt - lo) * 256 + qi * 128 + 128],
                                ba(pc_o, src_qi * L + mt * 128, 128),
                                reads=(("pc", src_qi),), writes=("ps2",))
                    dcopy(ba(pT_o, lo * 256, (hi - lo) * 256),
                          psb[2].ap()[:, 0:(hi - lo) * 256],
                          reads=("ps2",), writes=("pT",))
                first = True
                for mt in range(hi_mt + 1):
                    if mt <= 2 * g:
                        o_ap = psf[3].ap()[:, 0:256]
                        r_ap = ba(pT_o, mt * 256, 256)
                    else:
                        o_ap = psf[3].ap()[:, 128:256]
                        r_ap = ba(pT_o, mt * 256 + 128, 128)
                    mm(o_ap, ba(vtok_o, mt * 128, 128), r_ap,
                       first, mt == hi_mt,
                       reads=("vtok", "pT"), writes=("ps3",))
                    first = False
                av(ba(aT_o, g * 256, 256), psf[3].ap()[:, 0:256],
                   AF.Identity, reads=("ps3",), writes=("aTh",))
            ld(d_spill[h], ba(aT_o, 0, L), reads=("aTh",),
               writes=(("spill", h),))
        for nm in ("kcT", "vfm", "vtok", "qcT", "p1b", "p2b", "pcb", "t2b",
                   "pTb", "aTh", "ropet", "wA", "ckvn", "cqn"):
            AR.release(nm)

    if debug_phase != "A1":
        emit_a2_real()

    def emit_fold():
        CB = 128
        fa_o = AR.alloc("foldA", 66 * CB)
        fo_o = AR.alloc("foldO", 64 * CB // 2)
        for cb in range(D // CB):
            slot = list(range(64))
            spare = [64, 65]
            ld(fv(fa_o, 64 * CB).rearrange("p (c n) -> p c n", c=64),
               dm["wdownT"].rearrange("(c p) o -> p c o", p=128)
               [:, :, cb * CB:(cb + 1) * CB], writes=("foldA",))
            for si, hstep in enumerate((1, 2, 4, 8, 16, 32)):
                last = si == 5
                for a in range(64):
                    if a & hstep:
                        continue
                    b_ = a + hstep
                    sa, sb_ = slot[a], slot[b_]
                    if not last:
                        na, nb = spare
                        dtt(fv(fa_o + na * CB, CB), fv(fa_o + sa * CB, CB),
                            fv(fa_o + sb_ * CB, CB), ALU.add,
                            reads=("foldA",), writes=("foldA",))
                        dtt(fv(fa_o + nb * CB, CB), fv(fa_o + sa * CB, CB),
                            fv(fa_o + sb_ * CB, CB), ALU.subtract,
                            reads=("foldA",), writes=("foldA",))
                        spare = [sa, sb_]
                        slot[a], slot[b_] = na, nb
                    else:
                        dtt(arena_b.ap()[:, 2 * fo_o + a * CB:
                                         2 * fo_o + (a + 1) * CB],
                            fv(fa_o + sa * CB, CB), fv(fa_o + sb_ * CB, CB),
                            ALU.add, reads=("foldA",), writes=("foldO",))
                        dtt(arena_b.ap()[:, 2 * fo_o + b_ * CB:
                                         2 * fo_o + (b_ + 1) * CB],
                            fv(fa_o + sa * CB, CB), fv(fa_o + sb_ * CB, CB),
                            ALU.subtract, reads=("foldA",), writes=("foldO",))
            ld(d_wdp[:, :, cb * CB:(cb + 1) * CB].transpose([1, 0, 2]),
               arena_b.ap()[:, 2 * fo_o: 2 * fo_o + 64 * CB]
               .rearrange("p (c n) -> p c n", c=64),
               reads=("foldO",), writes=(("wdp_scr", cb),))
        AR.release("foldA")
        AR.release("foldO")

    def emit_a3(wname, spill_reads_tag, stream_src, stream_dst, wres_name):
        pb.phase_bar = pb.barrier_tokens()
        wo_o = AR.alloc("woutF", NC * L // 2)
        aT3_o = AR.alloc("aT3", NC * TT // 2)
        xres_o = AR.alloc("xres", NC * TT)
        snew_o = AR.alloc("snew3", NC * TT)
        if wname == "wout":
            src = dm["wout"].rearrange("(k p) o -> p k o", p=128)
        else:
            src = dm["wob"].transpose([1, 0, 2])
        ld(arena_b.ap()[:, 2 * wo_o: 2 * wo_o + NC * L]
           .rearrange("p (k o) -> p k o", k=NC), src, eng="pool",
           writes=(wres_name,))
        for t in range(NT):
            t0, t1 = t * TT, (t + 1) * TT
            ld(arena_b.ap()[:, 2 * aT3_o: 2 * aT3_o + NC * TT]
               .rearrange("p (h n) -> p h n", h=NH),
               d_spill[:, :, t0:t1].transpose([1, 0, 2]),
               reads=tuple((spill_reads_tag, h) for h in range(NH)),
               writes=("aT3",))
            if stream_src is None:
                ld(fv(xres_o, NC * TT).rearrange("p (c n) -> p c n", c=NC),
                   dx.rearrange("(c p) l -> p c l", p=128)[:, :, t0:t1],
                   writes=("xres",))
            else:
                ld(fv(xres_o, NC * TT).rearrange("p (c n) -> p c n", c=NC),
                   stream_src[:, :, t0:t1].transpose([1, 0, 2]),
                   reads=tuple(("stream", c, t) for c in range(NC)),
                   writes=("xres",))
            for oc in range(NC):
                bank = oc % 4
                mmgroup([(psf[bank].ap(),
                          arena_b.ap()[:, 2 * wo_o + k * L + oc * 128:
                                       2 * wo_o + k * L + oc * 128 + 128],
                          arena_b.ap()[:, 2 * aT3_o + k * TT:
                                       2 * aT3_o + (k + 1) * TT],
                          k == 0, k == NC - 1) for k in range(NC)],
                        reads=(wres_name, "aT3"), writes=(PS[bank],))
                dtt(fv(snew_o + oc * TT, TT), psf[bank].ap(),
                    fv(xres_o + oc * TT, TT), ALU.add,
                    reads=(PS[bank], "xres"), writes=("snew3",))
            ld(stream_dst[:, :, t0:t1].transpose([1, 0, 2]),
               fv(snew_o, NC * TT).rearrange("p (c n) -> p c n", c=NC),
               reads=("snew3",),
               writes=tuple(("stream", c, t) for c in range(NC)))
        for nm in ("woutF", "aT3", "xres", "snew3"):
            AR.release(nm)

    def emit_ffn(out_is_y):
        pb.phase_bar = pb.barrier_tokens()
        st_o = AR.alloc("fst", NC * TT)
        hzx_o = AR.alloc("hzx", NC * TT // 2)
        uz_o = AR.alloc("uz", NCI * TT // 2)
        wub_o = AR.alloc("wub", 2 * NC * 64)
        wgb_o = AR.alloc("wgb", 2 * NC * 64)
        wdb_o = AR.alloc("wdb", 2 * 512)
        ft_o = AR.alloc("ftmp", TT)
        silu_o = AR.alloc("fsilu", TT)

        def hz(c, n0, nn):
            return arena_b.ap()[:, 2 * hzx_o + c * TT + n0:
                                2 * hzx_o + c * TT + n0 + nn]

        def uzb(c2, n0, nn):
            return arena_b.ap()[:, 2 * uz_o + c2 * TT + n0:
                                2 * uz_o + c2 * TT + n0 + nn]

        for t in range(NT):
            t0, t1 = t * TT, (t + 1) * TT
            ld(fv(st_o, NC * TT).rearrange("p (c n) -> p c n", c=NC),
               d_stream[:, :, t0:t1].transpose([1, 0, 2]),
               reads=tuple(("stream", c, t) for c in range(NC)),
               writes=("fst",))
            xb2_o, sq2_o, mb2_o = uz_o, uz_o + 4096, uz_o + 8192
            ln_stats(st_o, TT, NC, 1.0 / D, EPS_LN, LNC_2048, "fst",
                     xb_at=xb2_o, sq_at=sq2_o, xtag="uz", stag="uz")
            bcast_row("r_mean", sb["r_mean"].ap()[:, 0:TT], TT, mb2_o, 6,
                      "uz")
            for sub in range(2):
                s0 = sub * 256
                for c in range(NC):
                    bank = 6 + (c % 2)
                    mm(psf[bank].ap()[:, 0:256],
                       sb["c_suug"].ap()[:, c * 128:(c + 1) * 128],
                       sb["r_rstd"].ap()[0:1, s0:s0 + 256], True, True,
                       reads=("r_rstd", "c_suug"), writes=(PS[bank],))
                    dtt(fv(silu_o, 256), fv(st_o + c * TT + s0, 256),
                        fv(mb2_o + s0, 256), ALU.subtract,
                        reads=("fst", "uz"), writes=("fsilu",))
                    dtt(fv(multi_o + c * 256, 256),
                        fv(silu_o, 256),
                        psf[bank].ap()[:, 0:256], ALU.mult,
                        reads=("fsilu", PS[bank]), writes=("bfly",))
                pos = [multi_o + i * 256 for i in range(NC)]
                spares = [multi_o + 16 * 256, ft_o, ft_o + 256]
                for si, hstep in enumerate((1, 2, 4, 8)):
                    last = si == 3
                    for a in range(NC):
                        if a & hstep:
                            continue
                        b_ = a + hstep
                        sa, sb_ = pos[a], pos[b_]
                        if not last:
                            na = spares.pop()
                            nb = spares.pop()
                            dtt(fv(na, 256), fv(sa, 256), fv(sb_, 256),
                                ALU.add, reads=("bfly",), writes=("bfly",))
                            dtt(fv(nb, 256), fv(sa, 256), fv(sb_, 256),
                                ALU.subtract, reads=("bfly",),
                                writes=("bfly",))
                            spares = [sa, sb_] + spares
                            pos[a], pos[b_] = na, nb
                        else:
                            dtt(hz(a, s0, 256), fv(sa, 256), fv(sb_, 256),
                                ALU.add, reads=("bfly",), writes=("hzx",))
                            dtt(hz(b_, s0, 256), fv(sa, 256), fv(sb_, 256),
                                ALU.subtract, reads=("bfly",),
                                writes=("hzx",))
            for c in range(NC):
                bank = c % 2
                mm(psf[bank].ap(), sb["c_h128"].ap(), hz(c, 0, TT),
                   True, True, reads=("hzx", "c_h128"), writes=(PS[bank],))
                av(hz(c, 0, TT), psf[bank].ap(), AF.Identity,
                   reads=(PS[bank],), writes=("hzx",))
            if DEBUG_PHASE == "F1h" and t == 0 and not out_is_y:
                oh_ = dbg("dbg_hx", [128, NC * TT], BF16)
                ld(oh_, arena_b.ap()[:, 2 * hzx_o: 2 * hzx_o + NC * TT],
                   reads=("hzx",), writes=("dbghx",))
            for c2 in range(NCI):
                slot = c2 % 2
                wu = wub_o + slot * NC * 64
                wg = wgb_o + slot * NC * 64
                ld(arena_b.ap()[:, 2 * wu: 2 * wu + NC * 128]
                   .rearrange("p (k n) -> p k n", k=NC),
                   dm["wupT"].rearrange("(k p) n -> p k n", p=128)
                   [:, :, c2 * 128:(c2 + 1) * 128], eng="pool",
                   writes=(("wub", slot),))
                ld(arena_b.ap()[:, 2 * wg: 2 * wg + NC * 128]
                   .rearrange("p (k n) -> p k n", k=NC),
                   dm["wgateT"].rearrange("(k p) n -> p k n", p=128)
                   [:, :, c2 * 128:(c2 + 1) * 128], eng="pool",
                   writes=(("wgb", slot),))
                bu = slot
                bg = 2 + slot
                bz = 4 + slot
                mmgroup([(psf[bu].ap(),
                          arena_b.ap()[:, 2 * wu + k * 128:
                                       2 * wu + (k + 1) * 128],
                          hz(k, 0, TT), k == 0, k == NC - 1)
                         for k in range(NC)],
                        reads=(("wub", slot), "hzx"), writes=(PS[bu],))
                mmgroup([(psf[bg].ap(),
                          arena_b.ap()[:, 2 * wg + k * 128:
                                       2 * wg + (k + 1) * 128],
                          hz(k, 0, TT), k == 0, k == NC - 1)
                         for k in range(NC)],
                        reads=(("wgb", slot), "hzx"), writes=(PS[bg],))
                av(fv(silu_o, TT), psf[bg].ap(), AF.Silu,
                   scale=sb["c_wsg64"].ap()[:, c2:c2 + 1],
                   reads=(PS[bg], "c_wsg64"), writes=("fsilu",))
                dstt(uzb(c2, 0, TT), psf[bu].ap(),
                     sb["c_wsups"].ap()[:, c2:c2 + 1], fv(silu_o, TT),
                     ALU.mult, ALU.mult,
                     reads=(PS[bu], "c_wsups", "fsilu"), writes=("uz",))
                mm(psf[bz].ap(), sb["c_h128"].ap(), uzb(c2, 0, TT),
                   True, True, reads=("uz", "c_h128"), writes=(PS[bz],))
                av(uzb(c2, 0, TT), psf[bz].ap(), AF.Identity, scale=C_8192,
                   reads=(PS[bz],), writes=("uz",))
            if DEBUG_PHASE == "F1h" and t == 0 and not out_is_y:
                oz_ = dbg("dbg_z", [128, NCI * TT], BF16)
                ld(oz_, arena_b.ap()[:, 2 * uz_o: 2 * uz_o + NCI * TT],
                   reads=("uz",), writes=("dbgz",))
            wdp_all = tuple(("wdp_scr", cb) for cb in range(D // 256))
            for oh in range(2):
                for c2 in range(NCI):
                    slot = c2 % 2
                    wd = wdb_o + slot * 512
                    ld(arena_b.ap()[:, 2 * wd: 2 * wd + 1024],
                       d_wdp[c2, :, oh * 1024:(oh + 1) * 1024],
                       eng="pool", reads=wdp_all, writes=(("wdb", slot),))
                    for oc8 in range(8):
                        mm(psf[oc8].ap(),
                           arena_b.ap()[:, 2 * wd + oc8 * 128:
                                        2 * wd + (oc8 + 1) * 128],
                           uzb(c2, 0, TT), c2 == 0, c2 == NCI - 1,
                           reads=(("wdb", slot), "uz"), writes=(PS[oc8],))
                for oc8 in range(8):
                    oc = oh * 8 + oc8
                    out_o = (hzx_o + oc8 * TT) if oh == 0 else \
                        (uz_o + oc8 * TT)
                    rtag = "hzx" if oh == 0 else "uz"
                    dstt(fv(out_o, TT), psf[oc8].ap(),
                         sb["c_wsdp64"].ap()[:, oc:oc + 1],
                         fv(st_o + oc * TT, TT), ALU.mult, ALU.add,
                         reads=(PS[oc8], "c_wsdp64", "fst"), writes=(rtag,))
            if out_is_y:
                dsts = [dy.rearrange("(c p) l -> p c l", p=128)
                        [:, 0:8, t0:t1],
                        dy.rearrange("(c p) l -> p c l", p=128)
                        [:, 8:16, t0:t1]]
                wr = [(("yout", t, 0),), (("yout", t, 1),)]
            else:
                dsts = [d_stream[0:8, :, t0:t1].transpose([1, 0, 2]),
                        d_stream[8:16, :, t0:t1].transpose([1, 0, 2])]
                wr = [tuple(("stream", c, t) for c in range(8)),
                      tuple(("stream", c, t) for c in range(8, 16))]
            ld(dsts[0], fv(hzx_o, 8 * TT).rearrange("p (c n) -> p c n", c=8),
               reads=("hzx",), writes=wr[0])
            ld(dsts[1], fv(uz_o, 8 * TT).rearrange("p (c n) -> p c n", c=8),
               reads=("uz",), writes=wr[1])
        for nm in ("fst", "hzx", "uz", "wub", "wgb", "wdb", "ftmp", "fsilu"):
            AR.release(nm)

    def emit_b():
        pb.phase_bar = pb.barrier_tokens()
        ln3_o = AR.alloc("ln3", NC * L // 2)
        st_o = AR.alloc("bst", NC * TT)
        xb3_o = AR.alloc("bxb", NC * TT // 2)
        sq3_o = AR.alloc("bsq", NC * TT // 2)
        mb3_o = AR.alloc("bmb", TT)
        rb3_o = AR.alloc("brb", TT)
        for t in range(NT):
            t0, t1 = t * TT, (t + 1) * TT
            ld(fv(st_o, NC * TT).rearrange("p (c n) -> p c n", c=NC),
               d_stream[:, :, t0:t1].transpose([1, 0, 2]),
               reads=tuple(("stream", c, t) for c in range(NC)),
               writes=("bst",))
            ln_stats(st_o, TT, NC, 1.0 / D, EPS_LN, 0.0, "bst",
                     xb_at=xb3_o, sq_at=sq3_o, xtag="bxb", stag="bsq")
            bcast_row("r_mean", sb["r_mean"].ap()[:, 0:TT], TT, mb3_o, 6,
                      "bmb")
            bcast_row("r_rstd", sb["r_rstd"].ap()[:, 0:TT], TT, rb3_o, 7,
                      "brb")
            st3 = fv(st_o, NC * TT).rearrange("p (c n) -> p c n", c=NC)
            mrep = bass.broadcast_tensor_aps(
                fv(mb3_o, TT).rearrange("p (o n) -> p o n", o=1), st3)[0]
            rrep = bass.broadcast_tensor_aps(
                fv(rb3_o, TT).rearrange("p (o n) -> p o n", o=1), st3)[0]
            dtt(st3, st3, mrep, ALU.subtract, reads=("bst", "bmb"),
                writes=("bst",))
            ln3v = arena_b.ap()[:, 2 * ln3_o: 2 * ln3_o + NC * L] \
                .rearrange("p (c l) -> p c l", c=NC)[:, :, t0:t1]
            dtt(ln3v, st3, rrep, ALU.mult, reads=("bst", "brb"),
                writes=("ln3",))
        if DEBUG_PHASE == "Bh":
            o8 = dbg("dbg_ln3", [128, NC * L], BF16)
            ld(o8, arena_b.ap()[:, 2 * ln3_o: 2 * ln3_o + NC * L],
               reads=("ln3",), writes=("dbgln3",))
            o8b = dbg("dbg_stream2", [NC, 128, L], F32)
            ld(o8b, d_stream, ew=pb.barrier_tokens(), writes=("dbgs2",))
        for nm in ("bst", "bxb", "bsq", "bmb", "brb"):
            AR.release(nm)

        def ln3c(c, n0, n1):
            return arena_b.ap()[:, 2 * ln3_o + c * L + n0:
                                2 * ln3_o + c * L + n1]

        wB_o = AR.alloc("wB", 3 * 1024)
        qT_o = AR.alloc("bqT", L // 2)
        kT_o = AR.alloc("bkT", L // 2)
        vf_o = AR.alloc("bvf", L // 2)
        vt_o = AR.alloc("bvt", L // 2)
        p1o = AR.alloc("bp1", L)
        pno = AR.alloc("bpn", L)
        pTo = AR.alloc("bpT", L)
        aTo = AR.alloc("baT", L // 2)
        for h in range(NH):
            for wi, wnm in enumerate(("wqb", "wkb", "wvb")):
                ld(arena_b.ap()[:, 2 * (wB_o + wi * 1024):
                                2 * (wB_o + wi * 1024) + NC * 128]
                   .rearrange("p (k n) -> p k n", k=NC),
                   dm[wnm][h].rearrange("(k p) n -> p k n", p=128),
                   eng="pool", writes=(("wB", wi),))
            for t in range(NT):
                t0, t1 = t * TT, (t + 1) * TT
                for wi, (dsto, tag, scl) in enumerate((
                        (qT_o, "bqT", f32(SC_B)), (kT_o, "bkT", f32(SC_B)),
                        (vf_o, "bvf", f32(1.0)))):
                    bank = 4 + wi
                    mmgroup([(psf[bank].ap(),
                              arena_b.ap()[:, 2 * (wB_o + wi * 1024) + k * 128:
                                           2 * (wB_o + wi * 1024) + (k + 1) * 128],
                              ln3c(k, t0, t1), k == 0, k == NC - 1)
                             for k in range(NC)],
                            reads=("ln3", ("wB", wi)), writes=(PS[bank],))
                    av(arena_b.ap()[:, 2 * dsto + t0: 2 * dsto + t1],
                       psf[bank].ap(), AF.Identity, scale=scl,
                       reads=(PS[bank],), writes=(tag,))
            for mt8 in range(2):
                for mi in range(8):
                    mt = mt8 * 8 + mi
                    trp(psb[2].ap()[:, mi * 128: mi * 128 + 128],
                        arena_b.ap()[:, 2 * vf_o + mt * 128:
                                     2 * vf_o + (mt + 1) * 128],
                        reads=("bvf",), writes=("ps2",))
                dcopy(arena_b.ap()[:, 2 * vt_o + mt8 * 1024:
                                   2 * vt_o + (mt8 + 1) * 1024],
                      psb[2].ap(), reads=("ps2",), writes=("bvt",))
            for g in range(NQT // 2):
                qts = (2 * g, 2 * g + 1)
                for qi, qt in enumerate(qts):
                    m_end = (qt + 1) * 128
                    nmb = (m_end + 511) // 512
                    for mb in range(nmb):
                        n = min(512, m_end - mb * 512)
                        bank = mb % 2
                        terms = [(psf[bank].ap()[:, 0:n],
                                  arena_b.ap()[:, 2 * qT_o + qt * 128:
                                               2 * qT_o + (qt + 1) * 128],
                                  arena_b.ap()[:, 2 * kT_o + mb * 512:
                                               2 * kT_o + mb * 512 + n],
                                  True, False)]
                        dg0 = qt * 128 - mb * 512
                        if 0 <= dg0 < n:
                            terms.append((psf[bank].ap()[:, dg0:dg0 + 128],
                                          sb["c_ident"].ap(),
                                          sb["c_masku"].ap(), False, True))
                        else:
                            o_, l_, r_, s_, _ = terms[-1]
                            terms[-1] = (o_, l_, r_, s_, True)
                        mmgroup(terms,
                                reads=("bqT", "bkT", "c_ident", "c_masku"),
                                writes=(PS[bank],))
                        sc_col = sb["scol"].ap()[:, qi * 4 + mb:
                                                 qi * 4 + mb + 1]
                        av(arena_b.ap()[:, 2 * p1o + qi * L + mb * 512:
                                        2 * p1o + qi * L + mb * 512 + n],
                           psf[bank].ap()[:, 0:n], AF.Exp, bias=EXP_C,
                           accum=sc_col, reads=(PS[bank],),
                           writes=("bp1", ("scol", 0, qi, mb)))
                for qi, qt in enumerate(qts):
                    nmb = ((qt + 1) * 128 + 511) // 512
                    dred(sb["rs1"].ap()[:, qi:qi + 1],
                         sb["scol"].ap()[:, qi * 4: qi * 4 + nmb], ALU.add,
                         reads=tuple(("scol", 0, qi, m) for m in range(nmb)),
                         writes=(("rs", 0, qi),))
                    drecip(sb["rs1"].ap()[:, qi:qi + 1],
                           sb["rs1"].ap()[:, qi:qi + 1],
                           reads=(("rs", 0, qi),), writes=(("rs", 0, qi),))
                    m_end = (qt + 1) * 128
                    dts(arena_b.ap()[:, 2 * pno + qi * L:
                                     2 * pno + qi * L + m_end],
                        arena_b.ap()[:, 2 * p1o + qi * L:
                                     2 * p1o + qi * L + m_end],
                        sb["rs1"].ap()[:, qi:qi + 1], None, ALU.mult,
                        reads=("bp1", ("rs", 0, qi)), writes=(("bpn", qi),))
                hi_mt = 2 * g + 1
                for mt4 in range((hi_mt + 4) // 4):
                    lo = mt4 * 4
                    hi = min(lo + 4, hi_mt + 1)
                    for mt in range(lo, hi):
                        for qi, qt in enumerate(qts):
                            src_qi = qi if mt <= qt else 1
                            trp(psb[2].ap()[:, (mt - lo) * 256 + qi * 128:
                                            (mt - lo) * 256 + qi * 128 + 128],
                                arena_b.ap()[:, 2 * pno + src_qi * L + mt * 128:
                                             2 * pno + src_qi * L + mt * 128 + 128],
                                reads=(("bpn", src_qi),), writes=("ps2",))
                    dcopy(arena_b.ap()[:, 2 * pTo + lo * 256:
                                       2 * pTo + lo * 256 + (hi - lo) * 256],
                          psb[2].ap()[:, 0:(hi - lo) * 256],
                          reads=("ps2",), writes=("bpT",))
                first = True
                for mt in range(hi_mt + 1):
                    if mt <= 2 * g:
                        o_ap = psf[3].ap()[:, 0:256]
                        r_ap = arena_b.ap()[:, 2 * pTo + mt * 256:
                                            2 * pTo + (mt + 1) * 256]
                    else:
                        o_ap = psf[3].ap()[:, 128:256]
                        r_ap = arena_b.ap()[:, 2 * pTo + mt * 256 + 128:
                                            2 * pTo + (mt + 1) * 256]
                    mm(o_ap, arena_b.ap()[:, 2 * vt_o + mt * 128:
                                          2 * vt_o + (mt + 1) * 128],
                       r_ap, first, mt == hi_mt,
                       reads=("bvt", "bpT"), writes=("ps3",))
                    first = False
                av(arena_b.ap()[:, 2 * aTo + g * 256: 2 * aTo + (g + 1) * 256],
                   psf[3].ap()[:, 0:256], AF.Identity, reads=("ps3",),
                   writes=("baT",))
            ld(d_spill[h], arena_b.ap()[:, 2 * aTo: 2 * aTo + L],
               reads=("baT",), writes=(("spill", h),))
        for nm in ("ln3", "wB", "bqT", "bkT", "bvf", "bvt", "bp1", "bpn",
                   "bpT", "baT"):
            AR.release(nm)

    if debug_phase not in ("A1", "A2"):
        emit_fold()
        emit_a3("wout", "spill", None, d_stream, "woutw")
    if debug_phase not in ("A1", "A2", "A3"):
        emit_ffn(False)
    if debug_phase not in ("A1", "A2", "A3", "F1", "F1h"):
        emit_b()
        if debug_phase == "Bh":
            o9 = dbg("dbg_bspill", [NH, 128, L], BF16)
            ld(o9, d_spill, ew=pb.barrier_tokens(), writes=("dbgbspill",))
        else:
            emit_a3("wob", "spill", d_stream, d_stream, "wobw")
    if debug_phase not in ("A1", "A2", "A3", "F1", "F1h", "B", "Bh"):
        emit_ffn(True)

    def dump_bf16(name, src_cols_off, nbcols, prows=128):
        o = dbg(name, [prows, nbcols], F32)
        bar = pb.barrier_tokens()
        stage = fv(xt_o, L, 0, prows)
        for i in range(0, nbcols, L):
            w = min(L, nbcols - i)
            dcopy(fv(xt_o, w, 0, prows),
                  arena_b.ap()[0:prows, src_cols_off + i: src_cols_off + i + w],
                  writes=("dbgstage",), ew=bar)
            ld(o[:, i:i + w], fv(xt_o, w, 0, prows), reads=("dbgstage",),
               writes=((name, i),))

    if debug_phase == "A1":
        dump_bf16("dbg_ckvn", 2 * ckvn_o, NCC * L)
        dump_bf16("dbg_cqn", 2 * cqn_o, NCC * L)
        o3 = dbg("dbg_krT", [64, L], F32)
        bar = pb.barrier_tokens()
        dcopy(fv(xt_o, L, 0, 64), sb["krT"].ap(), writes=("dbgstage",),
              ew=bar)
        ld(o3, fv(xt_o, L, 0, 64), reads=("dbgstage",), writes=("dbgkr",))
        o4 = dbg("dbg_lam", [128, NQT * NH], F32)
        ld(o4, sb["lam_sb"].ap(), ew=pb.barrier_tokens(), writes=("dbglam",))

    if debug_phase == "A2":
        o5 = dbg("dbg_spill", [NH, 128, L], BF16)
        ld(o5, d_spill, ew=pb.barrier_tokens(), writes=("dbgspill",))

    if debug_phase in ("A3", "F1", "B"):
        o6 = dbg("dbg_stream", [NC, 128, L], F32)
        ld(o6, d_stream, ew=pb.barrier_tokens(), writes=("dbgstream",))

    if debug_phase == "F1h":
        o7 = dbg("dbg_wdp", [NCI, 128, D], BF16)
        ld(o7, d_wdp, ew=pb.barrier_tokens(), writes=("dbgwdp",))

    pb.emit("sp", lambda e: e.nop(), extra_waits=pb.barrier_tokens())

    pb.flush()
    es.close()
    return nc, dbg_handles



def kernel(**inputs):
    consts = _host_consts()
    w = _prep_inputs(inputs)
    x = np.asarray(inputs["L0_attn_a_x"])
    nc, dbg_handles = build_program(DEBUG_PHASE)
    ncore = DEV_CORES if (DEBUG_PHASE and DEV_CORES) else NCORES
    in_maps = []
    for b in range(ncore):
        m = {"x_fm": np.ascontiguousarray(x[b].T)}
        for k, v in w.items():
            m[k] = np.ascontiguousarray(v, dtype=np.float32)
        for k in ("cos2", "sin2", "ones_row_f"):
            m[k] = consts[k]
        for k in ("h128", "ident", "masku", "ones_col_b"):
            m[k] = consts[k]
        in_maps.append(m)
    res = run_bass_kernel_spmd(nc, in_maps, core_ids=list(range(ncore)))
    if DEBUG_PHASE:
        return res
    out = np.empty((NCORES, L, D), dtype=np.float32)
    for b in range(NCORES):
        out[b] = res.results[b]["y_fm"].T
    return out


# version 02
# speedup vs baseline: 1.0036x; 1.0036x over previous
import math
from collections import defaultdict
from contextlib import ExitStack

import numpy as np
import ml_dtypes

import concourse.bass as bass
import concourse.mybir as mybir
from concourse.bass_utils import run_bass_kernel_spmd

F32 = mybir.dt.float32
BF16 = mybir.dt.bfloat16
AF = mybir.ActivationFunctionType
ALU = mybir.AluOpType
AX = mybir.AxisListType

f32 = np.float32

D = 2048
L = 2048
NCORES = 8
NH = 16
DH = 128
DHR = 64
DC = 1024
INTER = 8192
TT = 512
NT = L // TT
NC = D // 128
NCC = DC // 128
NCI = INTER // 128
NQT = L // 128
EPS_LN = f32(1e-5)
EPS_RMS = f32(1e-6)
EXP_C = f32(-12.0)
MASKVAL = -1e30

SC_A = float((1.0 / 192.0) ** 0.25)
SC_B = float((1.0 / 128.0) ** 0.25)
LNC_2048 = float(np.log(1.0 / (64.0 * math.sqrt(2048.0))))
C_8192 = f32(1.0 / (64.0 * math.sqrt(8192.0)))

DEBUG_PHASE = None
DEV_CORES = None
A1_TTILES = L // TT
A1_PARTS = ("ln", "lam", "kr", "ckv")
SERIALIZE = False
SER_PAIRS = ()
HWDGE_W = False



def _hadamard(n):
    h = np.array([[1.0]], dtype=np.float64)
    while h.shape[0] < n:
        h = np.block([[h, h], [h, -h]])
    return h


def _host_consts():
    inv_freq = 1.0 / 10000.0 ** (np.arange(0, DHR, 2, dtype=np.float64) / DHR)
    freqs = np.arange(L, dtype=np.float64)[:, None] * inv_freq[None, :]
    emb = np.concatenate([freqs, freqs], axis=-1)
    cos = np.cos(emb).T
    sin = np.sin(emb).T
    sinp = sin.copy()
    sinp[0:32] *= -1.0
    cos2 = np.concatenate([cos, cos], axis=0) * SC_A
    sin2 = np.concatenate([sinp, sinp], axis=0) * SC_A
    masku = np.triu(np.full((128, 128), MASKVAL, dtype=np.float64), k=1)
    return dict(
        cos2=np.ascontiguousarray(cos2, dtype=np.float32),
        sin2=np.ascontiguousarray(sin2, dtype=np.float32),
        h128=_hadamard(128).astype(ml_dtypes.bfloat16),
        ident=np.eye(128, dtype=ml_dtypes.bfloat16),
        masku=masku.astype(ml_dtypes.bfloat16),
        ones_col_b=np.ones((128, 1), dtype=ml_dtypes.bfloat16),
        ones_row_f=np.ones((1, 128), dtype=np.float32),
    )


def _col_swap_rope(w):
    v = w.reshape(w.shape[0], w.shape[1] // 64, 2, 32)[:, :, ::-1, :]
    return np.ascontiguousarray(v.reshape(w.shape))


def _prep_inputs(inp):
    c = {}
    c["wdkv"] = np.ascontiguousarray(inp["L0_attn_a_W_DKV"])
    c["wdq"] = np.ascontiguousarray(inp["L0_attn_a_W_DQ"])
    c["kvnw"] = np.ascontiguousarray(inp["L0_attn_a_kv_norm_w"].reshape(1, DC))
    c["qnw"] = np.ascontiguousarray(inp["L0_attn_a_q_norm_w"].reshape(1, DC))
    c["wuk"] = np.ascontiguousarray(inp["L0_attn_a_W_UK"])
    c["wuv"] = np.ascontiguousarray(inp["L0_attn_a_W_UV"])
    c["wuq"] = np.ascontiguousarray(inp["L0_attn_a_W_UQ"])
    c["wqr"] = np.ascontiguousarray(inp["L0_attn_a_W_QR"])
    c["wqrs"] = _col_swap_rope(inp["L0_attn_a_W_QR"])
    c["wkr"] = np.ascontiguousarray(inp["L0_attn_a_W_KR"])
    c["wkrs"] = _col_swap_rope(inp["L0_attn_a_W_KR"])
    c["wlam"] = np.ascontiguousarray(inp["L0_attn_a_W_lambda_w"])
    c["wlamb"] = np.ascontiguousarray(inp["L0_attn_a_W_lambda_b"].reshape(1, NH))
    c["wout"] = np.ascontiguousarray(inp["L0_attn_a_W_out"])
    c["wqb"] = np.ascontiguousarray(inp["L0_attn_b_W_Q"].transpose(0, 2, 1))
    c["wkb"] = np.ascontiguousarray(inp["L0_attn_b_W_K"].transpose(0, 2, 1))
    c["wvb"] = np.ascontiguousarray(inp["L0_attn_b_W_V"].transpose(0, 2, 1))
    c["wob"] = np.ascontiguousarray(inp["L0_attn_b_W_O"].transpose(0, 2, 1))
    c["suug"] = np.ascontiguousarray(inp["L0_ffn_SU_ug"].reshape(1, D))
    c["sudp"] = np.ascontiguousarray(inp["L0_ffn_SU_dp"].reshape(NCI, 128).T)
    c["wsup"] = np.ascontiguousarray(
        inp["L0_ffn_Wscale_ug"][:INTER].reshape(NCI, 128).T)
    c["wsgate"] = np.ascontiguousarray(
        inp["L0_ffn_Wscale_ug"][INTER:].reshape(NCI, 128).T)
    c["wsdp"] = np.ascontiguousarray(inp["L0_ffn_Wscale_dp"].reshape(NC, 128).T)
    c["wupT"] = np.ascontiguousarray(inp["L0_ffn_W_up"].T)
    c["wgateT"] = np.ascontiguousarray(inp["L0_ffn_W_gate"].T)
    c["wdownT"] = np.ascontiguousarray(inp["L0_ffn_W_down"].T)
    return c



ENGINES = ("pe", "act", "dve", "pool", "sp")


class Op:
    __slots__ = ("eng", "fns", "waits", "inc_needed", "val", "n", "gid",
                 "lane")

    def __init__(self, eng, fns, n, gid, lane):
        self.eng = eng
        self.fns = fns
        self.waits = []
        self.inc_needed = False
        self.val = None
        self.n = n
        self.gid = gid
        self.lane = lane


class PB:

    def __init__(self, nc, es):
        self.nc = nc
        self.es = es
        self.ops = {e: [] for e in ENGINES}
        self.writer = {}
        self.readers = defaultdict(list)
        self.gid = 0
        self.rr = defaultdict(int)
        self.last_op = None
        self.serialize_local = False
        self.phase_bar = ()
        self.seen_res = set()

    NLANES = 8

    def emit(self, eng, fns, reads=(), writes=(), extra_waits=(), lane=None):
        if callable(fns):
            fns = [fns]
        self.gid += 1
        is_dma = eng in ("sp", "pool")
        if is_dma:
            if lane is None:
                lane = self.rr[eng] % self.NLANES
                self.rr[eng] += 1
            lane_name = f"{eng}{lane}"
        else:
            lane_name = eng
        op = Op(eng, fns, 16 if is_dma else 1, self.gid, lane_name)
        deps = []
        for r in reads:
            w = self.writer.get(r)
            if w is not None:
                deps.append(w)
            if isinstance(r, str) and r.startswith("ps"):
                for rd in self.readers.get(r, ()):
                    if rd.eng != eng:
                        deps.append(rd)
        for r in writes:
            w = self.writer.get(r)
            if w is not None:
                deps.append(w)
            deps.extend(self.readers.get(r, ()))
        for d in extra_waits:
            if d is not None:
                deps.append(d)
        if (SERIALIZE or self.serialize_local) and self.last_op is not None:
            deps.append(self.last_op)
        for pa, pb_ in SER_PAIRS:
            other = pb_ if eng == pa else (pa if eng == pb_ else None)
            if other is not None and self.ops[other]:
                deps.append(self.ops[other][-1])
        for r in writes:
            if r not in self.seen_res:
                self.seen_res.add(r)
                deps.extend(self.phase_bar)
        seen = set()
        for d in deps:
            if d is op or id(d) in seen:
                continue
            seen.add(id(d))
            d.inc_needed = True
            op.waits.append(d)
        for r in reads:
            self.readers[r].append(op)
        for r in writes:
            self.writer[r] = op
            self.readers[r] = []
        self.ops[eng].append(op)
        self.last_op = op
        return op

    def barrier_tokens(self):
        outs = []
        for e in ENGINES:
            if self.ops[e]:
                outs.append(self.ops[e][-1])
        return outs

    def flush(self):
        nc = self.nc
        lanes = set()
        for e in ENGINES:
            if e in ("sp", "pool"):
                prev_on_lane = {}
                for op in self.ops[e]:
                    op.inc_needed = True
                    p = prev_on_lane.get(op.lane)
                    if p is not None and p not in op.waits:
                        op.waits.append(p)
                    prev_on_lane[op.lane] = op
            cum = defaultdict(int)
            for op in self.ops[e]:
                if op.inc_needed:
                    cum[op.lane] += op.n
                    op.val = cum[op.lane]
                    lanes.add(op.lane)
        sems = {ln: self.es.enter_context(nc.semaphore(f"sem_{ln}"))
                for ln in lanes}

        all_ops = sorted((op for e in ENGINES for op in self.ops[e]),
                        key=lambda o: o.gid)
        DMA_ENGS = ("sp", "pool")
        issue_imp = {}
        comp_imp = {}
        eng_prev = {}
        for op in all_ops:
            prev = eng_prev.get(op.eng)
            issue = dict(issue_imp[prev]) if prev is not None else {}
            need = {}
            for d in op.waits:
                assert d.val is not None, "dep without inc"
                if d.val > need.get(d.lane, (0, None))[0]:
                    need[d.lane] = (d.val, d)
            kept = {}
            for lane_, (v, d) in sorted(need.items()):
                ci = comp_imp[d]
                if issue.get(lane_, 0) < v:
                    kept[lane_] = v
                for l2, v2 in ci.items():
                    if issue.get(l2, 0) < v2:
                        issue[l2] = v2
                if issue.get(lane_, 0) < v:
                    issue[lane_] = v
            op.waits = kept
            issue_imp[op] = issue
            comp = dict(issue)
            if op.inc_needed:
                if comp.get(op.lane, 0) < op.val:
                    comp[op.lane] = op.val
            if prev is not None and op.eng not in DMA_ENGS:
                for l2, v2 in comp_imp[prev].items():
                    if comp.get(l2, 0) < v2:
                        comp[l2] = v2
            comp_imp[op] = comp
            eng_prev[op.eng] = op

        def runner(eng_name):
            eng_ops = self.ops[eng_name]

            def _run(eng):
                water = defaultdict(int)
                for op in eng_ops:
                    for s, v in op.waits.items():
                        if v > water[s]:
                            eng.wait_ge(sems[s], v)
                            water[s] = v
                    inst = None
                    for fn in op.fns:
                        inst = fn(eng)
                    if op.inc_needed:
                        inst.then_inc(sems[op.lane], op.n)
            return _run

        with nc.Block() as block:
            block.tensor(runner("pe"))
            block.scalar(runner("act"))
            block.vector(runner("dve"))
            block.gpsimd(runner("pool"))
            block.sync(runner("sp"))


class Arena:
    def __init__(self, total):
        self.free = [(0, total)]
        self.named = {}
        self.on_alloc = None

    def alloc(self, name, cols):
        for i, (off, sz) in enumerate(self.free):
            if sz >= cols:
                self.named[name] = (off, cols)
                if sz == cols:
                    self.free.pop(i)
                else:
                    self.free[i] = (off + cols, sz - cols)
                if self.on_alloc is not None:
                    self.on_alloc()
                return off
        raise MemoryError(f"arena OOM: {name} needs {cols} cols; free={self.free}")

    def release(self, name):
        off, cols = self.named.pop(name)
        self.free.append((off, cols))
        self.free.sort()
        merged = []
        for o, s in self.free:
            if merged and merged[-1][0] + merged[-1][1] == o:
                merged[-1] = (merged[-1][0], merged[-1][1] + s)
            else:
                merged.append((o, s))
        self.free = merged



def build_program(debug_phase=None):
    es = ExitStack()
    nc = bass.Bass("TRN2", target_bir_lowering=False)
    pb = PB(nc, es)

    dx = nc.dram_tensor("x_fm", [D, L], F32, kind="ExternalInput").ap()
    dm = {}
    for name, shape in [
        ("wdkv", (D, DC)), ("wdq", (D, DC)), ("kvnw", (1, DC)), ("qnw", (1, DC)),
        ("wuk", (DC, D)), ("wuv", (DC, D)), ("wuq", (DC, 2 * D)),
        ("wqr", (DC, D)), ("wqrs", (DC, D)), ("wkr", (D, DHR)), ("wkrs", (D, DHR)),
        ("wlam", (D, NH)), ("wlamb", (1, NH)), ("wout", (D, D)),
        ("wqb", (NH, D, DH)), ("wkb", (NH, D, DH)), ("wvb", (NH, D, DH)),
        ("wob", (NH, DH, D)),
        ("suug", (1, D)), ("sudp", (128, NCI)), ("wsup", (128, NCI)),
        ("wsgate", (128, NCI)), ("wsdp", (128, NC)),
        ("wupT", (D, INTER)), ("wgateT", (D, INTER)), ("wdownT", (INTER, D)),
        ("cos2", (128, L)), ("sin2", (128, L)), ("ones_row_f", (1, 128)),
    ]:
        dm[name] = nc.dram_tensor(name, list(shape), F32,
                                  kind="ExternalInput").ap()
    for name in ("h128", "ident", "masku", "ones_col_b"):
        shp = [128, 1] if name == "ones_col_b" else [128, 128]
        dm[name] = nc.dram_tensor(name, shp, BF16, kind="ExternalInput").ap()

    d_stream = nc.dram_tensor("scr_stream", [NC, 128, L], F32,
                              kind="Internal").ap()
    d_spill = nc.dram_tensor("scr_spill", [NH, 128, L], BF16,
                             kind="Internal").ap()
    d_wdp = nc.dram_tensor("scr_wdp", [NCI, 128, D], BF16, kind="Internal").ap()
    dy = nc.dram_tensor("y_fm", [D, L], F32, kind="ExternalOutput").ap()

    dbg_handles = {}

    def dbg(name, shape, dtype=F32):
        ap = nc.dram_tensor(name, list(shape), dtype, kind="ExternalOutput").ap()
        dbg_handles[name] = ap
        return ap

    sb = {}
    for name, shape, dt in [
        ("c_h128", [128, 128], BF16), ("c_ident", [128, 128], BF16),
        ("c_masku", [128, 128], BF16), ("c_onesb", [128, 1], BF16),
        ("c_onesf", [1, 128], F32), ("c_suug", [1, D], F32),
        ("c_kvnw", [1, DC], F32), ("c_qnw", [1, DC], F32),
        ("c_wlamb", [1, NH], F32),
        ("c_sudp", [128, NCI], F32), ("c_wsup", [128, NCI], F32),
        ("c_wsgate", [128, NCI], F32), ("c_wsdp", [128, NC], F32),
        ("c_wsg64", [128, NCI], F32), ("c_wsups", [128, NCI], F32),
        ("c_wsdp64", [128, NC], F32),
        ("r_mean", [1, TT], F32), ("r_msq", [1, TT], F32),
        ("r_var", [1, TT], F32), ("r_lnv", [1, TT], F32),
        ("r_rstd", [1, TT], F32), ("r_tmp", [1, TT], F32),
        ("lam_sb", [128, NQT * NH], F32),
        ("scol", [128, 64], F32), ("rs1", [128, 16], F32),
        ("rs2", [128, 16], F32), ("exp_t", [128, NH], F32),
        ("exp_t2", [128, NH], F32), ("lamb_b", [128, NH], F32),
        ("krT", [64, L], BF16),
        ("qrT0", [64, L], BF16), ("qrT1", [64, L], BF16),
    ]:
        sb[name] = nc.alloc_sbuf_tensor(name, shape, dt)

    psf = [nc.alloc_psum_tensor(f"ps{i}", [128, 512], F32) for i in range(8)]
    psb = [p.bitcast(BF16) for p in psf]
    PS = [f"ps{i}" for i in range(8)]

    rem = nc.sbuf_bytes_remaining
    total_cols = rem // 4 - 16
    arena = nc.alloc_sbuf_tensor("arena", [128, total_cols], F32)
    arena_b = arena.bitcast(BF16)
    AR = Arena(total_cols)

    def _on_alloc():
        pb.phase_bar = pb.barrier_tokens()
        pb.seen_res = set()
    AR.on_alloc = _on_alloc

    def fv(off, cols, p0=0, p1=128):
        return arena.ap()[p0:p1, off:off + cols]

    def bv(off, bcols, p0=0, p1=128):
        return arena_b.ap()[p0:p1, 2 * off:2 * off + bcols]

    def mm(out, lhsT, rhs, start, stop, reads=(), writes=(), ew=()):
        return pb.emit("pe", lambda e, o=out, l=lhsT, r=rhs, a=start, b=stop:
                       e.matmul(o, l, r, start=a, stop=b),
                       reads=reads, writes=writes, extra_waits=ew)

    def mmgroup(terms, reads=(), writes=(), ew=()):
        fns = [
            (lambda e, o=o, l=l, r=r, a=a, b=b: e.matmul(o, l, r, start=a,
                                                         stop=b))
            for (o, l, r, a, b) in terms]
        return pb.emit("pe", fns, reads=reads, writes=writes, extra_waits=ew)

    def trp(out, in_, reads=(), writes=(), ew=()):
        return pb.emit("pe", lambda e, o=out, i=in_, d=sb["c_ident"].ap():
                       e.transpose(o, i, d), reads=tuple(reads) + ("c_ident",),
                       writes=writes, extra_waits=ew)

    def av(out, in_, func, bias=f32(0), scale=f32(1), accum=None,
           reads=(), writes=(), ew=()):
        return pb.emit("act", lambda e, o=out, i=in_, f=func, b=bias, s=scale,
                       a=accum: e.activation(o, i, f, bias=b, scale=s,
                                             accum_out=a),
                       reads=reads, writes=writes, extra_waits=ew)

    def dcopy(out, in_, reads=(), writes=(), ew=()):
        return pb.emit("dve", lambda e, o=out, i=in_: e.tensor_copy(o, i),
                       reads=reads, writes=writes, extra_waits=ew)

    def dtt(out, a, b, op, reads=(), writes=(), ew=()):
        return pb.emit("dve", lambda e, o=out, x=a, y=b, p=op:
                       e.tensor_tensor(o, x, y, p),
                       reads=reads, writes=writes, extra_waits=ew)

    def dts(out, a, s1, s2, op0, op1=ALU.bypass, accum=None,
            reads=(), writes=(), ew=()):
        if isinstance(s1, (int, float, np.floating)):
            s1 = float(s1)
        if isinstance(s2, (int, float, np.floating)):
            s2 = float(s2)
        return pb.emit("dve", lambda e, o=out, x=a, u=s1, v=s2, p0=op0, p1=op1,
                       ac=accum: e.tensor_scalar(o, x, u, v, p0, p1,
                                                 accum_out=ac),
                       reads=reads, writes=writes, extra_waits=ew)

    def dstt(out, in0, scal, in1, op0, op1, reads=(), writes=(), ew=()):
        return pb.emit("dve", lambda e, o=out, a=in0, s=scal, b=in1, p0=op0,
                       p1=op1: e.scalar_tensor_tensor(o, a, s, b, p0, p1),
                       reads=reads, writes=writes, extra_waits=ew)

    def dred(out, in_, op, reads=(), writes=(), ew=()):
        return pb.emit("dve", lambda e, o=out, i=in_, p=op:
                       e.tensor_reduce(o, i, AX.X, p),
                       reads=reads, writes=writes, extra_waits=ew)

    def drecip(out, in_, reads=(), writes=(), ew=()):
        return pb.emit("dve", lambda e, o=out, i=in_: e.reciprocal(o, i),
                       reads=reads, writes=writes, extra_waits=ew)

    def ld(dst, src, reads=(), writes=(), eng="sp", ew=()):
        return pb.emit(eng, lambda e, d=dst, s=src: e.dma_start(d, s),
                       reads=reads, writes=writes, extra_waits=ew)

    for nm, src in [("c_h128", "h128"),
                    ("c_ident", "ident"), ("c_masku", "masku"),
                    ("c_onesb", "ones_col_b"), ("c_onesf", "ones_row_f"),
                    ("c_suug", "suug"), ("c_kvnw", "kvnw"), ("c_qnw", "qnw"),
                    ("c_wlamb", "wlamb"), ("c_sudp", "sudp"),
                    ("c_wsup", "wsup"), ("c_wsgate", "wsgate"),
                    ("c_wsdp", "wsdp")]:
        ld(sb[nm].ap(), dm[src], writes=(nm,))
    multi_o = AR.alloc("multi", 17 * 256)
    cos2_ap = fv(multi_o, L)
    sin2_ap = fv(multi_o + L, L)
    ld(cos2_ap, dm["cos2"], writes=("multi",))
    ld(sin2_ap, dm["sin2"], writes=("multi",))
    dts(sb["c_wsg64"].ap(), sb["c_wsgate"].ap(), f32(64.0), None, ALU.mult,
        reads=("c_wsgate",), writes=("c_wsg64",))
    dtt(sb["c_wsups"].ap(), sb["c_wsup"].ap(), sb["c_sudp"].ap(), ALU.mult,
        reads=("c_wsup", "c_sudp"), writes=("c_wsups",))
    dts(sb["c_wsups"].ap(), sb["c_wsups"].ap(), f32(64.0), None, ALU.mult,
        reads=("c_wsups",), writes=("c_wsups",))
    dts(sb["c_wsdp64"].ap(), sb["c_wsdp"].ap(), f32(64.0), None, ALU.mult,
        reads=("c_wsdp",), writes=("c_wsdp64",))

    onesb = sb["c_onesb"].ap()
    onesf = sb["c_onesf"].ap()

    mm(psf[5].ap()[:, 0:NH], onesf, sb["c_wlamb"].ap(), True, True,
       reads=("c_onesf", "c_wlamb"), writes=("ps5",))
    dcopy(sb["lamb_b"].ap(), psf[5].ap()[:, 0:NH], reads=("ps5",),
          writes=("lamb_b",))

    def ln_stats(src_o, tn, nch, inv_n, eps, extra_lnbias, tag, xb_at=None,
                 sq_at=None, xtag=None, stag=None):
        xo = xbln_o if xb_at is None else xb_at
        so = sq_o if sq_at is None else sq_at
        xtag = xtag or "xbln"
        stag = stag or "sq"
        xb_w = bv(xo, nch * tn)
        dcopy(xb_w, fv(src_o, nch * tn), reads=(tag,), writes=(xtag,))
        sq_w = bv(so, nch * tn)
        av(sq_w, xb_w, AF.Square, reads=(xtag,), writes=(stag,))
        stp1 = psf[6].ap()[0:1, 0:tn]
        stp2 = psf[7].ap()[0:1, 0:tn]
        mmgroup([(stp1, onesb, bv(xo + c * tn // 2, tn), c == 0,
                  c == nch - 1) for c in range(nch)],
                reads=(xtag, "c_onesb"), writes=("ps6",))
        mmgroup([(stp2, onesb, bv(so + c * tn // 2, tn), c == 0,
                  c == nch - 1) for c in range(nch)],
                reads=(stag, "c_onesb"), writes=("ps7",))
        r_mean = sb["r_mean"].ap()[:, 0:tn]
        r_msq = sb["r_msq"].ap()[:, 0:tn]
        r_var = sb["r_var"].ap()[:, 0:tn]
        r_lnv = sb["r_lnv"].ap()[:, 0:tn]
        r_rstd = sb["r_rstd"].ap()[:, 0:tn]
        r_tmp = sb["r_tmp"].ap()[:, 0:tn]
        dts(r_mean, stp1, f32(inv_n), None, ALU.mult, reads=("ps6",),
            writes=("r_mean",))
        dts(r_tmp, stp2, f32(inv_n), None, ALU.mult, reads=("ps7",),
            writes=("r_tmp",))
        av(r_msq, r_mean, AF.Square, reads=("r_mean",), writes=("r_msq",))
        dtt(r_var, r_tmp, r_msq, ALU.subtract, reads=("r_tmp", "r_msq"),
            writes=("r_var",))
        av(r_lnv, r_var, AF.Ln, bias=eps, reads=("r_var",), writes=("r_lnv",))
        av(r_rstd, r_lnv, AF.Exp, bias=f32(extra_lnbias), scale=f32(-0.5),
           reads=("r_lnv",), writes=("r_rstd",))

    def bcast_row(row_res, row_ap, tn, dst_off, bank, tag_dst, lhs_ap=None,
                  lhs_res=()):
        lhs = lhs_ap if lhs_ap is not None else onesf
        mm(psf[bank].ap()[:, 0:tn], lhs, row_ap, True, True,
           reads=tuple(lhs_res) + ("c_onesf", row_res), writes=(PS[bank],))
        dcopy(fv(dst_off, tn), psf[bank].ap()[:, 0:tn],
              reads=(PS[bank],), writes=(tag_dst,))

    ckvn_o = AR.alloc("ckvn", NCC * L // 2)
    cqn_o = AR.alloc("cqn", NCC * L // 2)

    def ckvn_ap(c, n0, n1):
        return arena_b.ap()[:, 2 * ckvn_o + c * L + n0: 2 * ckvn_o + c * L + n1]

    def cqn_ap(c, n0, n1):
        return arena_b.ap()[:, 2 * cqn_o + c * L + n0: 2 * cqn_o + c * L + n1]

    def krT_ap(n0, n1):
        return sb["krT"].ap()[:, n0:n1]

    xt_o = AR.alloc("xt", NC * TT)
    xbln_o = AR.alloc("xbln", NC * TT // 2)
    sq_o = AR.alloc("sq", NC * TT // 2)
    ckvf_o = xt_o
    wstr_o = AR.alloc("wstr", 2 * DC // 2)
    wstrf_o = AR.alloc("wstrf", DC) if HWDGE_W else None
    wkr_o = AR.alloc("wkr", NC * DHR)
    wlamA_o = AR.alloc("wlamA", NC * NH // 2)
    mb_o = AR.alloc("meanb", TT)
    rb_o = AR.alloc("rstdb", TT)
    tmpA_o = AR.alloc("tmpA", 2 * TT)

    def ln_bf(c, n0=0, n1=TT):
        return arena_b.ap()[:, 2 * xbln_o + c * TT + n0:
                            2 * xbln_o + c * TT + n1]

    ld(bv(wkr_o, NC * DHR).rearrange("p (c r) -> p c r", c=NC),
       dm["wkr"].rearrange("(c p) r -> p c r", p=128),
       eng="pool", writes=("wkr",))
    ld(arena_b.ap()[:, 2 * wkr_o + NC * DHR: 2 * wkr_o + 2 * NC * DHR]
       .rearrange("p (c r) -> p c r", c=NC),
       dm["wkrs"].rearrange("(c p) r -> p c r", p=128), eng="pool",
       writes=("wkrs",))
    ld(bv(wlamA_o, NC * NH).rearrange("p (c h) -> p c h", c=NC),
       dm["wlam"].rearrange("(c p) h -> p c h", p=128),
       eng="pool", writes=("wlamA",))

    def emit_a1():
        for t in range(A1_TTILES):
            t0, t1 = t * TT, (t + 1) * TT
            ld(fv(xt_o, NC * TT).rearrange("p (c n) -> p c n", c=NC),
               dx.rearrange("(c p) l -> p c l", p=128)[:, :, t0:t1],
               writes=("xt",))
            ln_stats(xt_o, TT, NC, 1.0 / D, EPS_LN, 0.0, "xt")
            bcast_row("r_mean", sb["r_mean"].ap()[:, 0:TT], TT, mb_o, 6,
                      "meanb")
            bcast_row("r_rstd", sb["r_rstd"].ap()[:, 0:TT], TT, rb_o, 7,
                      "rstdb")
            xt3 = fv(xt_o, NC * TT).rearrange("p (c n) -> p c n", c=NC)
            xb3 = bv(xbln_o, NC * TT).rearrange("p (c n) -> p c n", c=NC)
            mrep = bass.broadcast_tensor_aps(
                fv(mb_o, TT).rearrange("p (o n) -> p o n", o=1), xt3)[0]
            rrep = bass.broadcast_tensor_aps(
                fv(rb_o, TT).rearrange("p (o n) -> p o n", o=1), xt3)[0]
            dtt(xt3, xt3, mrep, ALU.subtract, reads=("xt", "meanb"),
                writes=("xt",))
            dtt(xb3, xt3, rrep, ALU.mult, reads=("xt", "rstdb"),
                writes=("xbln",))
            lam_ps = psf[5].ap()[:, 0:NH]
            for qt in range(4 if "lam" in A1_PARTS else 0):
                q0 = qt * 128
                terms = [(lam_ps, ln_bf(c, q0, q0 + 128),
                          bv(wlamA_o + c * NH // 2, NH), c == 0, c == NC - 1)
                         for c in range(NC)]
                mmgroup(terms, reads=("xbln", "wlamA"), writes=("ps5",))
                dtt(sb["exp_t"].ap(), lam_ps, sb["lamb_b"].ap(), ALU.add,
                    reads=("ps5", "lamb_b"), writes=("exp_t",))
                av(sb["exp_t2"].ap(), sb["exp_t"].ap(), AF.Exp, scale=f32(-1),
                   reads=("exp_t",), writes=("exp_t2",))
                qi = t * 4 + qt
                lam_cols = sb["lam_sb"].ap()[:, qi * NH:(qi + 1) * NH]
                dts(lam_cols, sb["exp_t2"].ap(), f32(1.0), None, ALU.add,
                    reads=("exp_t2",), writes=(("lam", qi),))
                drecip(lam_cols, lam_cols, reads=(("lam", qi),),
                       writes=(("lam", qi),))
            if "kr" not in A1_PARTS:
                continue
            krp = psf[4].ap()[0:64, 0:TT]
            krps = psf[5].ap()[0:64, 0:TT]
            mmgroup([(krp,
                      arena_b.ap()[:, 2 * wkr_o + c * DHR:
                                   2 * wkr_o + (c + 1) * DHR],
                      ln_bf(c), c == 0, c == NC - 1) for c in range(NC)],
                    reads=("xbln", "wkr"), writes=("ps4",))
            mmgroup([(krps,
                      arena_b.ap()[:, 2 * wkr_o + NC * DHR + c * DHR:
                                   2 * wkr_o + NC * DHR + (c + 1) * DHR],
                      ln_bf(c), c == 0, c == NC - 1) for c in range(NC)],
                    reads=("xbln", "wkrs"), writes=("ps5",))
            if "kr_nodve" in A1_PARTS:
                continue
            ta = fv(tmpA_o, TT, 0, 64)
            tb = fv(tmpA_o + TT, TT, 0, 64)
            if "kr_nomulti" in A1_PARTS:
                cosk = fv(mb_o, TT, 0, 64)
                sink = fv(mb_o, TT, 0, 64)
            else:
                cosk = fv(multi_o + t0, TT, 0, 64)
                sink = fv(multi_o + L + t0, TT, 0, 64)
            pb.serialize_local = "kr_ser" in A1_PARTS
            if "kr_nopshift" in A1_PARTS:
                t1_ = pb.emit("dve", lambda e: e.engine_nop())
                t1_.inc_needed = True
                t2_ = pb.emit("dve", lambda e: e.engine_nop())
                t2_.inc_needed = True
                t3_ = pb.emit("act", lambda e: e.nop())
                t3_.inc_needed = True
                pb.serialize_local = False
                continue
            if "kr_act" in A1_PARTS:
                ea = fv(tmpA_o, TT, 0, 64)
                eb = fv(tmpA_o + TT, TT, 0, 64)
                av(ea, krp, AF.Identity, reads=("ps4",), writes=("tmpA_lo",))
                av(eb, krps, AF.Identity, reads=("ps5",), writes=("tmpA_hi",))
                if "kr_actonly" in A1_PARTS:
                    pb.serialize_local = False
                    continue
                dtt(ea, ea, cosk, ALU.mult, reads=("tmpA_lo", "multi"),
                    writes=("tmpA_lo",))
                dtt(eb, eb, sink, ALU.mult, reads=("tmpA_hi", "multi"),
                    writes=("tmpA_hi",))
                if "kr_no3" in A1_PARTS:
                    pb.serialize_local = False
                    continue
                dtt(ea, ea, eb, ALU.add, reads=("tmpA_lo", "tmpA_hi"),
                    writes=("tmpA_lo",))
                if "kr_addonly" in A1_PARTS:
                    pb.serialize_local = False
                    continue
                av(krT_ap(t0, t1), ea, AF.Identity, reads=("tmpA_lo",),
                   writes=("krT",))
            else:
                dtt(ta, krp, cosk, ALU.mult, reads=("ps4", "multi"),
                    writes=("tmpA_lo",))
                dtt(tb, krps, sink, ALU.mult, reads=("ps5", "multi"),
                    writes=("tmpA_hi",))
                dtt(krT_ap(t0, t1), ta, tb, ALU.add,
                    reads=("tmpA_lo", "tmpA_hi"), writes=("krT",))
            if "kr_pad" in A1_PARTS:
                pb.emit("dve", lambda e: e.engine_nop(),
                        reads=(), writes=("ps4", "ps5"))
            pb.serialize_local = False
            if "ckv" not in A1_PARTS:
                continue
            for wname, nrow, dstfn, dsttag in (
                ("wdkv", "c_kvnw", ckvn_ap, "ckvn"),
                ("wdq", "c_qnw", cqn_ap, "cqn"),
            ):
                for k in range(NC):
                    slot = k % 2
                    woff = wstr_o + slot * DC // 2
                    wres = ("wstr", slot)
                    if HWDGE_W:
                        fres = ("wstrf", 0)
                        foff = wstrf_o
                        ld(fv(foff, DC), dm[wname][k * 128:(k + 1) * 128, :],
                           eng="sp", writes=(fres,))
                        av(bv(woff, DC), fv(foff, DC), AF.Identity,
                           reads=(fres,), writes=(wres,))
                    else:
                        ld(bv(woff, DC), dm[wname][k * 128:(k + 1) * 128, :],
                           eng="pool", writes=(wres,))
                    if "ckv_dma_only" in A1_PARTS:
                        continue
                    mmgroup([(psf[oc].ap(), bv(woff + oc * 64, 128), ln_bf(k),
                              k == 0, k == NC - 1) for oc in range(NCC)],
                            reads=("xbln", wres),
                            writes=tuple(PS[i] for i in range(NCC)))
                if "ckv_mm_only" in A1_PARTS:
                    continue
                for oc in range(NCC):
                    dcopy(fv(ckvf_o + oc * TT, TT), psf[oc].ap(),
                          reads=(PS[oc],), writes=("xt",))
                    av(bv(sq_o + oc * TT // 2, TT), fv(ckvf_o + oc * TT, TT),
                       AF.Square, reads=("xt",), writes=(("sqo", oc),))
                if "ckv_evac_only" in A1_PARTS:
                    continue
                mmgroup([(psf[6].ap()[0:1, 0:TT], onesb,
                          bv(sq_o + oc * TT // 2, TT), oc == 0, oc == NCC - 1)
                         for oc in range(NCC)],
                        reads=tuple(("sqo", oc) for oc in range(NCC))
                        + ("c_onesb",), writes=("ps6",))
                r_var = sb["r_var"].ap()[:, 0:TT]
                r_lnv = sb["r_lnv"].ap()[:, 0:TT]
                r_rstd = sb["r_rstd"].ap()[:, 0:TT]
                dts(r_var, psf[6].ap()[0:1, 0:TT], f32(1.0 / DC), None,
                    ALU.mult, reads=("ps6",), writes=("r_var",))
                av(r_lnv, r_var, AF.Ln, bias=EPS_RMS, reads=("r_var",),
                   writes=("r_lnv",))
                av(r_rstd, r_lnv, AF.Exp, scale=f32(-0.5), reads=("r_lnv",),
                   writes=("r_rstd",))
                for oc in range(NCC):
                    mm(psf[oc].ap(), sb[nrow].ap()[:, oc * 128:(oc + 1) * 128],
                       sb["r_rstd"].ap()[:, 0:TT], True, True,
                       reads=("r_rstd", nrow), writes=(PS[oc],))
                for oc in range(NCC):
                    dtt(dstfn(oc, t0, t1), fv(ckvf_o + oc * TT, TT),
                        psf[oc].ap(), ALU.mult,
                        reads=("xt", PS[oc]), writes=(dsttag,))
        for nm in ("wstr", "wlamA", "wkr"):
            AR.release(nm)

    emit_a1()

    def emit_a2_real():
        for nm in (("xt", "xbln", "sq", "wstrf", "meanb", "rstdb", "tmpA")
                   if HWDGE_W else
                   ("xt", "xbln", "sq", "meanb", "rstdb", "tmpA")):
            AR.release(nm)
        pb.phase_bar = pb.barrier_tokens()

        kcT_o = AR.alloc("kcT", L // 2)
        vfm_o = AR.alloc("vfm", L // 2)
        vtok_o = AR.alloc("vtok", L // 2)
        qcT_o = AR.alloc("qcT", L)
        p1_o = AR.alloc("p1b", L)
        p2_o = AR.alloc("p2b", L)
        pc_o = AR.alloc("pcb", L)
        t2_o = AR.alloc("t2b", L // 2)
        pT_o = AR.alloc("pTb", L)
        aT_o = AR.alloc("aTh", L // 2)
        rt_o = AR.alloc("ropet", 2 * TT)
        wA_o = AR.alloc("wA", 3072)

        def ba(off_f32, bcol0, bn, p0=0, p1=128):
            c = 2 * off_f32 + bcol0
            return arena_b.ap()[p0:p1, c:c + bn]

        for h in range(NH):
            slot = 0
            so = wA_o
            wuk_s, wuv_s, wuq_s, wqr_s, wqrs_s = (so, so + 512, so + 1024,
                                                  so + 2048, so + 2560)
            res_w = ("wA", slot)
            for nm, off, cols in (("wuk", wuk_s, 128), ("wuv", wuv_s, 128),
                                  ("wuq", wuq_s, 256), ("wqr", wqr_s, 128),
                                  ("wqrs", wqrs_s, 128)):
                src = dm[nm].rearrange("(k p) n -> p k n", p=128)
                c0 = h * cols
                ld(ba(off, 0, NCC * cols).rearrange("p (k n) -> p k n", k=NCC),
                   src[:, :, c0:c0 + cols], eng="pool", writes=(res_w,))
            for t in range(NT):
                t0, t1 = t * TT, (t + 1) * TT
                mmgroup([(psf[4].ap(), ba(wuk_s, k * 128, 128),
                          ckvn_ap(k, t0, t1), k == 0, k == NCC - 1)
                         for k in range(NCC)],
                        reads=("ckvn", res_w), writes=("ps4",))
                av(ba(kcT_o, t0, TT), psf[4].ap(), AF.Identity,
                   scale=f32(SC_A), reads=("ps4",), writes=("kcT",))
                mmgroup([(psf[5].ap(), ba(wuv_s, k * 128, 128),
                          ckvn_ap(k, t0, t1), k == 0, k == NCC - 1)
                         for k in range(NCC)],
                        reads=("ckvn", res_w), writes=("ps5",))
                av(ba(vfm_o, t0, TT), psf[5].ap(), AF.Identity,
                   reads=("ps5",), writes=("vfm",))
                for p_ in range(2):
                    mmgroup([(psf[6 + p_].ap(), ba(wuq_s, k * 256 + p_ * 128, 128),
                              cqn_ap(k, t0, t1), k == 0, k == NCC - 1)
                             for k in range(NCC)],
                            reads=("cqn", res_w), writes=(PS[6 + p_],))
                    av(ba(qcT_o, p_ * L + t0, TT), psf[6 + p_].ap(),
                       AF.Identity, scale=f32(SC_A),
                       reads=(PS[6 + p_],), writes=("qcT",))
                for p_ in range(2):
                    qps = psf[4].ap()[0:64, 0:TT]
                    qpss = psf[5].ap()[0:64, 0:TT]
                    mmgroup([(qps, ba(wqr_s, k * 128 + p_ * 64, 64),
                              cqn_ap(k, t0, t1), k == 0, k == NCC - 1)
                             for k in range(NCC)],
                            reads=("cqn", res_w), writes=("ps4",))
                    mmgroup([(qpss, ba(wqrs_s, k * 128 + p_ * 64, 64),
                              cqn_ap(k, t0, t1), k == 0, k == NCC - 1)
                             for k in range(NCC)],
                            reads=("cqn", res_w), writes=("ps5",))
                    ra = fv(rt_o, TT, 0, 64)
                    rb2 = fv(rt_o + TT, TT, 0, 64)
                    cosq = fv(multi_o + t0, TT, 0, 64)
                    sinq = fv(multi_o + L + t0, TT, 0, 64)
                    dtt(ra, qps, cosq, ALU.mult, reads=("ps4", "multi"),
                        writes=("ropea",))
                    dtt(rb2, qpss, sinq, ALU.mult, reads=("ps5", "multi"),
                        writes=("ropeb",))
                    qrt = (sb["qrT0"] if p_ == 0 else sb["qrT1"]).ap()
                    dtt(qrt[:, t0:t1], ra, rb2, ALU.add,
                        reads=("ropea", "ropeb"), writes=(("qrT", p_),))
            for mt8 in range(2):
                for mi in range(8):
                    mt = mt8 * 8 + mi
                    trp(psb[2].ap()[:, mi * 128: mi * 128 + 128],
                        ba(vfm_o, mt * 128, 128),
                        reads=("vfm",), writes=("ps2",))
                dcopy(ba(vtok_o, mt8 * 1024, 1024), psb[2].ap(),
                      reads=("ps2",), writes=("vtok",))
            for g in range(NQT // 2):
                qts = (2 * g, 2 * g + 1)
                for p_ in range(2):
                    pbuf_o = p1_o if p_ == 0 else p2_o
                    pres = ("p1" if p_ == 0 else "p2")
                    qrt = (sb["qrT0"] if p_ == 0 else sb["qrT1"]).ap()
                    for qi, qt in enumerate(qts):
                        m_end = (qt + 1) * 128
                        nmb = (m_end + 511) // 512
                        for mb in range(nmb):
                            n = min(512, m_end - mb * 512)
                            bank = mb % 2
                            terms = [
                                (psf[bank].ap()[:, 0:n],
                                 ba(qcT_o, p_ * L + qt * 128, 128),
                                 ba(kcT_o, mb * 512, n), True, False),
                                (psf[bank].ap()[:, 0:n],
                                 qrt[:, qt * 128:(qt + 1) * 128],
                                 sb["krT"].ap()[:, mb * 512: mb * 512 + n],
                                 False, False),
                            ]
                            dg0 = qt * 128 - mb * 512
                            if 0 <= dg0 < n:
                                terms.append(
                                    (psf[bank].ap()[:, dg0:dg0 + 128],
                                     sb["c_ident"].ap(), sb["c_masku"].ap(),
                                     False, True))
                            else:
                                o_, l_, r_, s_, _ = terms[-1]
                                terms[-1] = (o_, l_, r_, s_, True)
                            mmgroup(terms,
                                    reads=("qcT", ("qrT", p_), "kcT", "krT",
                                           "c_ident", "c_masku"),
                                    writes=(PS[bank],))
                            sc_col = sb["scol"].ap()[:, p_ * 8 + qi * 4 + mb:
                                                     p_ * 8 + qi * 4 + mb + 1]
                            av(ba(pbuf_o, qi * L + mb * 512, n),
                               psf[bank].ap()[:, 0:n], AF.Exp, bias=EXP_C,
                               accum=sc_col, reads=(PS[bank],),
                               writes=(pres, ("scol", p_, qi, mb)))
                for p_ in range(2):
                    rsb = (sb["rs1"] if p_ == 0 else sb["rs2"]).ap()
                    for qi, qt in enumerate(qts):
                        nmb = ((qt + 1) * 128 + 511) // 512
                        sc0 = p_ * 8 + qi * 4
                        dred(rsb[:, qi:qi + 1],
                             sb["scol"].ap()[:, sc0:sc0 + nmb], ALU.add,
                             reads=tuple(("scol", p_, qi, m)
                                         for m in range(nmb)),
                             writes=(("rs", p_, qi),))
                        drecip(rsb[:, qi:qi + 1], rsb[:, qi:qi + 1],
                               reads=(("rs", p_, qi),),
                               writes=(("rs", p_, qi),))
                for qi, qt in enumerate(qts):
                    m_end = (qt + 1) * 128
                    lam_col = sb["lam_sb"].ap()[:, qt * NH + h: qt * NH + h + 1]
                    dts(ba(t2_o, 0, m_end), ba(p2_o, qi * L, m_end),
                        lam_col, sb["rs2"].ap()[:, qi:qi + 1], ALU.mult,
                        ALU.mult,
                        reads=("p2", ("rs", 1, qi), ("lam", qt)),
                        writes=("t2b",))
                    dstt(ba(pc_o, qi * L, m_end), ba(p1_o, qi * L, m_end),
                         sb["rs1"].ap()[:, qi:qi + 1], ba(t2_o, 0, m_end),
                         ALU.mult, ALU.subtract,
                         reads=("p1", ("rs", 0, qi), "t2b"),
                         writes=(("pc", qi),))
                hi_mt = 2 * g + 1
                for mt4 in range((hi_mt + 4) // 4):
                    lo = mt4 * 4
                    hi = min(lo + 4, hi_mt + 1)
                    for mt in range(lo, hi):
                        for qi, qt in enumerate(qts):
                            src_qi = qi if mt <= qt else 1
                            trp(psb[2].ap()[:, (mt - lo) * 256 + qi * 128:
                                            (mt - lo) * 256 + qi * 128 + 128],
                                ba(pc_o, src_qi * L + mt * 128, 128),
                                reads=(("pc", src_qi),), writes=("ps2",))
                    dcopy(ba(pT_o, lo * 256, (hi - lo) * 256),
                          psb[2].ap()[:, 0:(hi - lo) * 256],
                          reads=("ps2",), writes=("pT",))
                first = True
                for mt in range(hi_mt + 1):
                    if mt <= 2 * g:
                        o_ap = psf[3].ap()[:, 0:256]
                        r_ap = ba(pT_o, mt * 256, 256)
                    else:
                        o_ap = psf[3].ap()[:, 128:256]
                        r_ap = ba(pT_o, mt * 256 + 128, 128)
                    mm(o_ap, ba(vtok_o, mt * 128, 128), r_ap,
                       first, mt == hi_mt,
                       reads=("vtok", "pT"), writes=("ps3",))
                    first = False
                av(ba(aT_o, g * 256, 256), psf[3].ap()[:, 0:256],
                   AF.Identity, reads=("ps3",), writes=("aTh",))
            ld(d_spill[h], ba(aT_o, 0, L), reads=("aTh",),
               writes=(("spill", h),))
        for nm in ("kcT", "vfm", "vtok", "qcT", "p1b", "p2b", "pcb", "t2b",
                   "pTb", "aTh", "ropet", "wA", "ckvn", "cqn"):
            AR.release(nm)

    if debug_phase != "A1":
        emit_a2_real()

    def emit_fold():
        CB = 128
        fa_o = AR.alloc("foldA", 66 * CB)
        fo_o = AR.alloc("foldO", 64 * CB // 2)
        for cb in range(D // CB):
            slot = list(range(64))
            spare = [64, 65]
            ld(fv(fa_o, 64 * CB).rearrange("p (c n) -> p c n", c=64),
               dm["wdownT"].rearrange("(c p) o -> p c o", p=128)
               [:, :, cb * CB:(cb + 1) * CB], writes=("foldA",))
            for si, hstep in enumerate((1, 2, 4, 8, 16, 32)):
                last = si == 5
                for a in range(64):
                    if a & hstep:
                        continue
                    b_ = a + hstep
                    sa, sb_ = slot[a], slot[b_]
                    if not last:
                        na, nb = spare
                        dtt(fv(fa_o + na * CB, CB), fv(fa_o + sa * CB, CB),
                            fv(fa_o + sb_ * CB, CB), ALU.add,
                            reads=("foldA",), writes=("foldA",))
                        dtt(fv(fa_o + nb * CB, CB), fv(fa_o + sa * CB, CB),
                            fv(fa_o + sb_ * CB, CB), ALU.subtract,
                            reads=("foldA",), writes=("foldA",))
                        spare = [sa, sb_]
                        slot[a], slot[b_] = na, nb
                    else:
                        dtt(arena_b.ap()[:, 2 * fo_o + a * CB:
                                         2 * fo_o + (a + 1) * CB],
                            fv(fa_o + sa * CB, CB), fv(fa_o + sb_ * CB, CB),
                            ALU.add, reads=("foldA",), writes=("foldO",))
                        dtt(arena_b.ap()[:, 2 * fo_o + b_ * CB:
                                         2 * fo_o + (b_ + 1) * CB],
                            fv(fa_o + sa * CB, CB), fv(fa_o + sb_ * CB, CB),
                            ALU.subtract, reads=("foldA",), writes=("foldO",))
            ld(d_wdp[:, :, cb * CB:(cb + 1) * CB].transpose([1, 0, 2]),
               arena_b.ap()[:, 2 * fo_o: 2 * fo_o + 64 * CB]
               .rearrange("p (c n) -> p c n", c=64),
               reads=("foldO",), writes=(("wdp_scr", cb),))
        AR.release("foldA")
        AR.release("foldO")

    def emit_a3(wname, spill_reads_tag, stream_src, stream_dst, wres_name):
        pb.phase_bar = pb.barrier_tokens()
        wo_o = AR.alloc("woutF", NC * L // 2)
        aT3_o = AR.alloc("aT3", NC * TT // 2)
        xres_o = AR.alloc("xres", NC * TT)
        snew_o = AR.alloc("snew3", NC * TT)
        if wname == "wout":
            src = dm["wout"].rearrange("(k p) o -> p k o", p=128)
        else:
            src = dm["wob"].transpose([1, 0, 2])
        ld(arena_b.ap()[:, 2 * wo_o: 2 * wo_o + NC * L]
           .rearrange("p (k o) -> p k o", k=NC), src, eng="pool",
           writes=(wres_name,))
        for t in range(NT):
            t0, t1 = t * TT, (t + 1) * TT
            ld(arena_b.ap()[:, 2 * aT3_o: 2 * aT3_o + NC * TT]
               .rearrange("p (h n) -> p h n", h=NH),
               d_spill[:, :, t0:t1].transpose([1, 0, 2]),
               reads=tuple((spill_reads_tag, h) for h in range(NH)),
               writes=("aT3",))
            if stream_src is None:
                ld(fv(xres_o, NC * TT).rearrange("p (c n) -> p c n", c=NC),
                   dx.rearrange("(c p) l -> p c l", p=128)[:, :, t0:t1],
                   writes=("xres",))
            else:
                ld(fv(xres_o, NC * TT).rearrange("p (c n) -> p c n", c=NC),
                   stream_src[:, :, t0:t1].transpose([1, 0, 2]),
                   reads=tuple(("stream", c, t) for c in range(NC)),
                   writes=("xres",))
            for oc in range(NC):
                bank = oc % 4
                mmgroup([(psf[bank].ap(),
                          arena_b.ap()[:, 2 * wo_o + k * L + oc * 128:
                                       2 * wo_o + k * L + oc * 128 + 128],
                          arena_b.ap()[:, 2 * aT3_o + k * TT:
                                       2 * aT3_o + (k + 1) * TT],
                          k == 0, k == NC - 1) for k in range(NC)],
                        reads=(wres_name, "aT3"), writes=(PS[bank],))
                dtt(fv(snew_o + oc * TT, TT), psf[bank].ap(),
                    fv(xres_o + oc * TT, TT), ALU.add,
                    reads=(PS[bank], "xres"), writes=("snew3",))
            ld(stream_dst[:, :, t0:t1].transpose([1, 0, 2]),
               fv(snew_o, NC * TT).rearrange("p (c n) -> p c n", c=NC),
               reads=("snew3",),
               writes=tuple(("stream", c, t) for c in range(NC)))
        for nm in ("woutF", "aT3", "xres", "snew3"):
            AR.release(nm)

    def emit_ffn(out_is_y):
        pb.phase_bar = pb.barrier_tokens()
        st_o = AR.alloc("fst", NC * TT)
        hzx_o = AR.alloc("hzx", NC * TT // 2)
        uz_o = AR.alloc("uz", NCI * TT // 2)
        wub_o = AR.alloc("wub", 2 * NC * 64)
        wgb_o = AR.alloc("wgb", 2 * NC * 64)
        wdb_o = AR.alloc("wdb", 2 * 512)
        ft_o = AR.alloc("ftmp", TT)
        silu_o = AR.alloc("fsilu", TT)

        def hz(c, n0, nn):
            return arena_b.ap()[:, 2 * hzx_o + c * TT + n0:
                                2 * hzx_o + c * TT + n0 + nn]

        def uzb(c2, n0, nn):
            return arena_b.ap()[:, 2 * uz_o + c2 * TT + n0:
                                2 * uz_o + c2 * TT + n0 + nn]

        for t in range(NT):
            t0, t1 = t * TT, (t + 1) * TT
            ld(fv(st_o, NC * TT).rearrange("p (c n) -> p c n", c=NC),
               d_stream[:, :, t0:t1].transpose([1, 0, 2]),
               reads=tuple(("stream", c, t) for c in range(NC)),
               writes=("fst",))
            xb2_o, sq2_o, mb2_o = uz_o, uz_o + 4096, uz_o + 8192
            ln_stats(st_o, TT, NC, 1.0 / D, EPS_LN, LNC_2048, "fst",
                     xb_at=xb2_o, sq_at=sq2_o, xtag="uz", stag="uz")
            bcast_row("r_mean", sb["r_mean"].ap()[:, 0:TT], TT, mb2_o, 6,
                      "uz")
            for sub in range(2):
                s0 = sub * 256
                for c in range(NC):
                    bank = 6 + (c % 2)
                    mm(psf[bank].ap()[:, 0:256],
                       sb["c_suug"].ap()[:, c * 128:(c + 1) * 128],
                       sb["r_rstd"].ap()[0:1, s0:s0 + 256], True, True,
                       reads=("r_rstd", "c_suug"), writes=(PS[bank],))
                    dtt(fv(silu_o, 256), fv(st_o + c * TT + s0, 256),
                        fv(mb2_o + s0, 256), ALU.subtract,
                        reads=("fst", "uz"), writes=("fsilu",))
                    dtt(fv(multi_o + c * 256, 256),
                        fv(silu_o, 256),
                        psf[bank].ap()[:, 0:256], ALU.mult,
                        reads=("fsilu", PS[bank]), writes=("bfly",))
                pos = [multi_o + i * 256 for i in range(NC)]
                spares = [multi_o + 16 * 256, ft_o, ft_o + 256]
                for si, hstep in enumerate((1, 2, 4, 8)):
                    last = si == 3
                    for a in range(NC):
                        if a & hstep:
                            continue
                        b_ = a + hstep
                        sa, sb_ = pos[a], pos[b_]
                        if not last:
                            na = spares.pop()
                            nb = spares.pop()
                            dtt(fv(na, 256), fv(sa, 256), fv(sb_, 256),
                                ALU.add, reads=("bfly",), writes=("bfly",))
                            dtt(fv(nb, 256), fv(sa, 256), fv(sb_, 256),
                                ALU.subtract, reads=("bfly",),
                                writes=("bfly",))
                            spares = [sa, sb_] + spares
                            pos[a], pos[b_] = na, nb
                        else:
                            dtt(hz(a, s0, 256), fv(sa, 256), fv(sb_, 256),
                                ALU.add, reads=("bfly",), writes=("hzx",))
                            dtt(hz(b_, s0, 256), fv(sa, 256), fv(sb_, 256),
                                ALU.subtract, reads=("bfly",),
                                writes=("hzx",))
            for c in range(NC):
                bank = c % 2
                mm(psf[bank].ap(), sb["c_h128"].ap(), hz(c, 0, TT),
                   True, True, reads=("hzx", "c_h128"), writes=(PS[bank],))
                av(hz(c, 0, TT), psf[bank].ap(), AF.Identity,
                   reads=(PS[bank],), writes=("hzx",))
            if DEBUG_PHASE == "F1h" and t == 0 and not out_is_y:
                oh_ = dbg("dbg_hx", [128, NC * TT], BF16)
                ld(oh_, arena_b.ap()[:, 2 * hzx_o: 2 * hzx_o + NC * TT],
                   reads=("hzx",), writes=("dbghx",))
            for c2 in range(NCI):
                slot = c2 % 2
                wu = wub_o + slot * NC * 64
                wg = wgb_o + slot * NC * 64
                ld(arena_b.ap()[:, 2 * wu: 2 * wu + NC * 128]
                   .rearrange("p (k n) -> p k n", k=NC),
                   dm["wupT"].rearrange("(k p) n -> p k n", p=128)
                   [:, :, c2 * 128:(c2 + 1) * 128], eng="pool",
                   writes=(("wub", slot),))
                ld(arena_b.ap()[:, 2 * wg: 2 * wg + NC * 128]
                   .rearrange("p (k n) -> p k n", k=NC),
                   dm["wgateT"].rearrange("(k p) n -> p k n", p=128)
                   [:, :, c2 * 128:(c2 + 1) * 128], eng="pool",
                   writes=(("wgb", slot),))
                bu = slot
                bg = 2 + slot
                bz = 4 + slot
                mmgroup([(psf[bu].ap(),
                          arena_b.ap()[:, 2 * wu + k * 128:
                                       2 * wu + (k + 1) * 128],
                          hz(k, 0, TT), k == 0, k == NC - 1)
                         for k in range(NC)],
                        reads=(("wub", slot), "hzx"), writes=(PS[bu],))
                mmgroup([(psf[bg].ap(),
                          arena_b.ap()[:, 2 * wg + k * 128:
                                       2 * wg + (k + 1) * 128],
                          hz(k, 0, TT), k == 0, k == NC - 1)
                         for k in range(NC)],
                        reads=(("wgb", slot), "hzx"), writes=(PS[bg],))
                av(fv(silu_o, TT), psf[bg].ap(), AF.Silu,
                   scale=sb["c_wsg64"].ap()[:, c2:c2 + 1],
                   reads=(PS[bg], "c_wsg64"), writes=("fsilu",))
                dstt(uzb(c2, 0, TT), psf[bu].ap(),
                     sb["c_wsups"].ap()[:, c2:c2 + 1], fv(silu_o, TT),
                     ALU.mult, ALU.mult,
                     reads=(PS[bu], "c_wsups", "fsilu"), writes=("uz",))
                mm(psf[bz].ap(), sb["c_h128"].ap(), uzb(c2, 0, TT),
                   True, True, reads=("uz", "c_h128"), writes=(PS[bz],))
                av(uzb(c2, 0, TT), psf[bz].ap(), AF.Identity, scale=C_8192,
                   reads=(PS[bz],), writes=("uz",))
            if DEBUG_PHASE == "F1h" and t == 0 and not out_is_y:
                oz_ = dbg("dbg_z", [128, NCI * TT], BF16)
                ld(oz_, arena_b.ap()[:, 2 * uz_o: 2 * uz_o + NCI * TT],
                   reads=("uz",), writes=("dbgz",))
            wdp_all = tuple(("wdp_scr", cb) for cb in range(D // 256))
            for oh in range(2):
                for c2 in range(NCI):
                    slot = c2 % 2
                    wd = wdb_o + slot * 512
                    ld(arena_b.ap()[:, 2 * wd: 2 * wd + 1024],
                       d_wdp[c2, :, oh * 1024:(oh + 1) * 1024],
                       eng="pool", reads=wdp_all, writes=(("wdb", slot),))
                    for oc8 in range(8):
                        mm(psf[oc8].ap(),
                           arena_b.ap()[:, 2 * wd + oc8 * 128:
                                        2 * wd + (oc8 + 1) * 128],
                           uzb(c2, 0, TT), c2 == 0, c2 == NCI - 1,
                           reads=(("wdb", slot), "uz"), writes=(PS[oc8],))
                for oc8 in range(8):
                    oc = oh * 8 + oc8
                    out_o = (hzx_o + oc8 * TT) if oh == 0 else \
                        (uz_o + oc8 * TT)
                    rtag = "hzx" if oh == 0 else "uz"
                    dstt(fv(out_o, TT), psf[oc8].ap(),
                         sb["c_wsdp64"].ap()[:, oc:oc + 1],
                         fv(st_o + oc * TT, TT), ALU.mult, ALU.add,
                         reads=(PS[oc8], "c_wsdp64", "fst"), writes=(rtag,))
            if out_is_y:
                dsts = [dy.rearrange("(c p) l -> p c l", p=128)
                        [:, 0:8, t0:t1],
                        dy.rearrange("(c p) l -> p c l", p=128)
                        [:, 8:16, t0:t1]]
                wr = [(("yout", t, 0),), (("yout", t, 1),)]
            else:
                dsts = [d_stream[0:8, :, t0:t1].transpose([1, 0, 2]),
                        d_stream[8:16, :, t0:t1].transpose([1, 0, 2])]
                wr = [tuple(("stream", c, t) for c in range(8)),
                      tuple(("stream", c, t) for c in range(8, 16))]
            ld(dsts[0], fv(hzx_o, 8 * TT).rearrange("p (c n) -> p c n", c=8),
               reads=("hzx",), writes=wr[0])
            ld(dsts[1], fv(uz_o, 8 * TT).rearrange("p (c n) -> p c n", c=8),
               reads=("uz",), writes=wr[1])
        for nm in ("fst", "hzx", "uz", "wub", "wgb", "wdb", "ftmp", "fsilu"):
            AR.release(nm)

    def emit_b():
        pb.phase_bar = pb.barrier_tokens()
        ln3_o = AR.alloc("ln3", NC * L // 2)
        st_o = AR.alloc("bst", NC * TT)
        xb3_o = AR.alloc("bxb", NC * TT // 2)
        sq3_o = AR.alloc("bsq", NC * TT // 2)
        mb3_o = AR.alloc("bmb", TT)
        rb3_o = AR.alloc("brb", TT)
        for t in range(NT):
            t0, t1 = t * TT, (t + 1) * TT
            ld(fv(st_o, NC * TT).rearrange("p (c n) -> p c n", c=NC),
               d_stream[:, :, t0:t1].transpose([1, 0, 2]),
               reads=tuple(("stream", c, t) for c in range(NC)),
               writes=("bst",))
            ln_stats(st_o, TT, NC, 1.0 / D, EPS_LN, 0.0, "bst",
                     xb_at=xb3_o, sq_at=sq3_o, xtag="bxb", stag="bsq")
            bcast_row("r_mean", sb["r_mean"].ap()[:, 0:TT], TT, mb3_o, 6,
                      "bmb")
            bcast_row("r_rstd", sb["r_rstd"].ap()[:, 0:TT], TT, rb3_o, 7,
                      "brb")
            st3 = fv(st_o, NC * TT).rearrange("p (c n) -> p c n", c=NC)
            mrep = bass.broadcast_tensor_aps(
                fv(mb3_o, TT).rearrange("p (o n) -> p o n", o=1), st3)[0]
            rrep = bass.broadcast_tensor_aps(
                fv(rb3_o, TT).rearrange("p (o n) -> p o n", o=1), st3)[0]
            dtt(st3, st3, mrep, ALU.subtract, reads=("bst", "bmb"),
                writes=("bst",))
            ln3v = arena_b.ap()[:, 2 * ln3_o: 2 * ln3_o + NC * L] \
                .rearrange("p (c l) -> p c l", c=NC)[:, :, t0:t1]
            dtt(ln3v, st3, rrep, ALU.mult, reads=("bst", "brb"),
                writes=("ln3",))
        if DEBUG_PHASE == "Bh":
            o8 = dbg("dbg_ln3", [128, NC * L], BF16)
            ld(o8, arena_b.ap()[:, 2 * ln3_o: 2 * ln3_o + NC * L],
               reads=("ln3",), writes=("dbgln3",))
            o8b = dbg("dbg_stream2", [NC, 128, L], F32)
            ld(o8b, d_stream, ew=pb.barrier_tokens(), writes=("dbgs2",))
        for nm in ("bst", "bxb", "bsq", "bmb", "brb"):
            AR.release(nm)

        def ln3c(c, n0, n1):
            return arena_b.ap()[:, 2 * ln3_o + c * L + n0:
                                2 * ln3_o + c * L + n1]

        wB_o = AR.alloc("wB", 3 * 1024)
        qT_o = AR.alloc("bqT", L // 2)
        kT_o = AR.alloc("bkT", L // 2)
        vf_o = AR.alloc("bvf", L // 2)
        vt_o = AR.alloc("bvt", L // 2)
        p1o = AR.alloc("bp1", L)
        pno = AR.alloc("bpn", L)
        pTo = AR.alloc("bpT", L)
        aTo = AR.alloc("baT", L // 2)
        for h in range(NH):
            for wi, wnm in enumerate(("wqb", "wkb", "wvb")):
                ld(arena_b.ap()[:, 2 * (wB_o + wi * 1024):
                                2 * (wB_o + wi * 1024) + NC * 128]
                   .rearrange("p (k n) -> p k n", k=NC),
                   dm[wnm][h].rearrange("(k p) n -> p k n", p=128),
                   eng="pool", writes=(("wB", wi),))
            for t in range(NT):
                t0, t1 = t * TT, (t + 1) * TT
                for wi, (dsto, tag, scl) in enumerate((
                        (qT_o, "bqT", f32(SC_B)), (kT_o, "bkT", f32(SC_B)),
                        (vf_o, "bvf", f32(1.0)))):
                    bank = 4 + wi
                    mmgroup([(psf[bank].ap(),
                              arena_b.ap()[:, 2 * (wB_o + wi * 1024) + k * 128:
                                           2 * (wB_o + wi * 1024) + (k + 1) * 128],
                              ln3c(k, t0, t1), k == 0, k == NC - 1)
                             for k in range(NC)],
                            reads=("ln3", ("wB", wi)), writes=(PS[bank],))
                    av(arena_b.ap()[:, 2 * dsto + t0: 2 * dsto + t1],
                       psf[bank].ap(), AF.Identity, scale=scl,
                       reads=(PS[bank],), writes=(tag,))
            for mt8 in range(2):
                for mi in range(8):
                    mt = mt8 * 8 + mi
                    trp(psb[2].ap()[:, mi * 128: mi * 128 + 128],
                        arena_b.ap()[:, 2 * vf_o + mt * 128:
                                     2 * vf_o + (mt + 1) * 128],
                        reads=("bvf",), writes=("ps2",))
                dcopy(arena_b.ap()[:, 2 * vt_o + mt8 * 1024:
                                   2 * vt_o + (mt8 + 1) * 1024],
                      psb[2].ap(), reads=("ps2",), writes=("bvt",))
            for g in range(NQT // 2):
                qts = (2 * g, 2 * g + 1)
                for qi, qt in enumerate(qts):
                    m_end = (qt + 1) * 128
                    nmb = (m_end + 511) // 512
                    for mb in range(nmb):
                        n = min(512, m_end - mb * 512)
                        bank = mb % 2
                        terms = [(psf[bank].ap()[:, 0:n],
                                  arena_b.ap()[:, 2 * qT_o + qt * 128:
                                               2 * qT_o + (qt + 1) * 128],
                                  arena_b.ap()[:, 2 * kT_o + mb * 512:
                                               2 * kT_o + mb * 512 + n],
                                  True, False)]
                        dg0 = qt * 128 - mb * 512
                        if 0 <= dg0 < n:
                            terms.append((psf[bank].ap()[:, dg0:dg0 + 128],
                                          sb["c_ident"].ap(),
                                          sb["c_masku"].ap(), False, True))
                        else:
                            o_, l_, r_, s_, _ = terms[-1]
                            terms[-1] = (o_, l_, r_, s_, True)
                        mmgroup(terms,
                                reads=("bqT", "bkT", "c_ident", "c_masku"),
                                writes=(PS[bank],))
                        sc_col = sb["scol"].ap()[:, qi * 4 + mb:
                                                 qi * 4 + mb + 1]
                        av(arena_b.ap()[:, 2 * p1o + qi * L + mb * 512:
                                        2 * p1o + qi * L + mb * 512 + n],
                           psf[bank].ap()[:, 0:n], AF.Exp, bias=EXP_C,
                           accum=sc_col, reads=(PS[bank],),
                           writes=("bp1", ("scol", 0, qi, mb)))
                for qi, qt in enumerate(qts):
                    nmb = ((qt + 1) * 128 + 511) // 512
                    dred(sb["rs1"].ap()[:, qi:qi + 1],
                         sb["scol"].ap()[:, qi * 4: qi * 4 + nmb], ALU.add,
                         reads=tuple(("scol", 0, qi, m) for m in range(nmb)),
                         writes=(("rs", 0, qi),))
                    drecip(sb["rs1"].ap()[:, qi:qi + 1],
                           sb["rs1"].ap()[:, qi:qi + 1],
                           reads=(("rs", 0, qi),), writes=(("rs", 0, qi),))
                    m_end = (qt + 1) * 128
                    dts(arena_b.ap()[:, 2 * pno + qi * L:
                                     2 * pno + qi * L + m_end],
                        arena_b.ap()[:, 2 * p1o + qi * L:
                                     2 * p1o + qi * L + m_end],
                        sb["rs1"].ap()[:, qi:qi + 1], None, ALU.mult,
                        reads=("bp1", ("rs", 0, qi)), writes=(("bpn", qi),))
                hi_mt = 2 * g + 1
                for mt4 in range((hi_mt + 4) // 4):
                    lo = mt4 * 4
                    hi = min(lo + 4, hi_mt + 1)
                    for mt in range(lo, hi):
                        for qi, qt in enumerate(qts):
                            src_qi = qi if mt <= qt else 1
                            trp(psb[2].ap()[:, (mt - lo) * 256 + qi * 128:
                                            (mt - lo) * 256 + qi * 128 + 128],
                                arena_b.ap()[:, 2 * pno + src_qi * L + mt * 128:
                                             2 * pno + src_qi * L + mt * 128 + 128],
                                reads=(("bpn", src_qi),), writes=("ps2",))
                    dcopy(arena_b.ap()[:, 2 * pTo + lo * 256:
                                       2 * pTo + lo * 256 + (hi - lo) * 256],
                          psb[2].ap()[:, 0:(hi - lo) * 256],
                          reads=("ps2",), writes=("bpT",))
                first = True
                for mt in range(hi_mt + 1):
                    if mt <= 2 * g:
                        o_ap = psf[3].ap()[:, 0:256]
                        r_ap = arena_b.ap()[:, 2 * pTo + mt * 256:
                                            2 * pTo + (mt + 1) * 256]
                    else:
                        o_ap = psf[3].ap()[:, 128:256]
                        r_ap = arena_b.ap()[:, 2 * pTo + mt * 256 + 128:
                                            2 * pTo + (mt + 1) * 256]
                    mm(o_ap, arena_b.ap()[:, 2 * vt_o + mt * 128:
                                          2 * vt_o + (mt + 1) * 128],
                       r_ap, first, mt == hi_mt,
                       reads=("bvt", "bpT"), writes=("ps3",))
                    first = False
                av(arena_b.ap()[:, 2 * aTo + g * 256: 2 * aTo + (g + 1) * 256],
                   psf[3].ap()[:, 0:256], AF.Identity, reads=("ps3",),
                   writes=("baT",))
            ld(d_spill[h], arena_b.ap()[:, 2 * aTo: 2 * aTo + L],
               reads=("baT",), writes=(("spill", h),))
        for nm in ("ln3", "wB", "bqT", "bkT", "bvf", "bvt", "bp1", "bpn",
                   "bpT", "baT"):
            AR.release(nm)

    if debug_phase not in ("A1", "A2"):
        emit_fold()
        emit_a3("wout", "spill", None, d_stream, "woutw")
    if debug_phase not in ("A1", "A2", "A3"):
        emit_ffn(False)
    if debug_phase not in ("A1", "A2", "A3", "F1", "F1h"):
        emit_b()
        if debug_phase == "Bh":
            o9 = dbg("dbg_bspill", [NH, 128, L], BF16)
            ld(o9, d_spill, ew=pb.barrier_tokens(), writes=("dbgbspill",))
        else:
            emit_a3("wob", "spill", d_stream, d_stream, "wobw")
    if debug_phase not in ("A1", "A2", "A3", "F1", "F1h", "B", "Bh"):
        emit_ffn(True)

    def dump_bf16(name, src_cols_off, nbcols, prows=128):
        o = dbg(name, [prows, nbcols], F32)
        bar = pb.barrier_tokens()
        stage = fv(xt_o, L, 0, prows)
        for i in range(0, nbcols, L):
            w = min(L, nbcols - i)
            dcopy(fv(xt_o, w, 0, prows),
                  arena_b.ap()[0:prows, src_cols_off + i: src_cols_off + i + w],
                  writes=("dbgstage",), ew=bar)
            ld(o[:, i:i + w], fv(xt_o, w, 0, prows), reads=("dbgstage",),
               writes=((name, i),))

    if debug_phase == "A1":
        dump_bf16("dbg_ckvn", 2 * ckvn_o, NCC * L)
        dump_bf16("dbg_cqn", 2 * cqn_o, NCC * L)
        o3 = dbg("dbg_krT", [64, L], F32)
        bar = pb.barrier_tokens()
        dcopy(fv(xt_o, L, 0, 64), sb["krT"].ap(), writes=("dbgstage",),
              ew=bar)
        ld(o3, fv(xt_o, L, 0, 64), reads=("dbgstage",), writes=("dbgkr",))
        o4 = dbg("dbg_lam", [128, NQT * NH], F32)
        ld(o4, sb["lam_sb"].ap(), ew=pb.barrier_tokens(), writes=("dbglam",))

    if debug_phase == "A2":
        o5 = dbg("dbg_spill", [NH, 128, L], BF16)
        ld(o5, d_spill, ew=pb.barrier_tokens(), writes=("dbgspill",))

    if debug_phase in ("A3", "F1", "B"):
        o6 = dbg("dbg_stream", [NC, 128, L], F32)
        ld(o6, d_stream, ew=pb.barrier_tokens(), writes=("dbgstream",))

    if debug_phase == "F1h":
        o7 = dbg("dbg_wdp", [NCI, 128, D], BF16)
        ld(o7, d_wdp, ew=pb.barrier_tokens(), writes=("dbgwdp",))

    pb.emit("sp", lambda e: e.nop(), extra_waits=pb.barrier_tokens())

    pb.flush()
    es.close()
    return nc, dbg_handles



def kernel(**inputs):
    consts = _host_consts()
    w = _prep_inputs(inputs)
    x = np.asarray(inputs["L0_attn_a_x"])
    nc, dbg_handles = build_program(DEBUG_PHASE)
    ncore = DEV_CORES if (DEBUG_PHASE and DEV_CORES) else NCORES
    in_maps = []
    for b in range(ncore):
        m = {"x_fm": np.ascontiguousarray(x[b].T)}
        for k, v in w.items():
            m[k] = np.ascontiguousarray(v, dtype=np.float32)
        for k in ("cos2", "sin2", "ones_row_f"):
            m[k] = consts[k]
        for k in ("h128", "ident", "masku", "ones_col_b"):
            m[k] = consts[k]
        in_maps.append(m)
    res = run_bass_kernel_spmd(nc, in_maps, core_ids=list(range(ncore)))
    if DEBUG_PHASE:
        return res
    out = np.empty((NCORES, L, D), dtype=np.float32)
    for b in range(NCORES):
        out[b] = res.results[b]["y_fm"].T
    return out
